# version 08
# baseline (speedup 1.0000x reference)
import numpy as np
import ml_dtypes
from contextlib import ExitStack

import concourse.bass as bass
import concourse.mybir as mybir
from concourse.bass_utils import run_bass_kernel_spmd

F32 = mybir.dt.float32
BF16 = mybir.dt.bfloat16
U32 = mybir.dt.uint32
AF = mybir.ActivationFunctionType
ALU = mybir.AluOpType
AX = mybir.AxisListType

N_CORES = 8
TPB = 128
NT = 16
TOK = TPB * NT
DM = 1024
KC = 8
NP = 8
NE = 8
FCG = 16
LN_EPS = 1e-5
Z_EPS = float(np.finfo(np.float32).eps)
THR = 0.7

C_LNEPS, C_ZEPS, C_THR, C_ONE, C_IOTA0, C_PIDX, C_CSEL0 = 0, 1, 2, 3, 4, 12, 13
C_NCOLS = 24


class Prog:
    ENGINES = ("tensor", "vector", "scalar", "gpsimd", "sync")

    def __init__(self, strict=frozenset()):
        self.steps = {e: [] for e in self.ENGINES}
        self.counts = {e: 0 for e in self.ENGINES}
        self.counts["cc"] = 0
        self.strict = frozenset(strict) if not isinstance(strict, bool) else (
            frozenset(self.ENGINES) if strict else frozenset())
        self.last_ev = {e: None for e in self.ENGINES}

    def step(self, eng, fn, deps=(), signal=False, dma=False, cc=False, tag=""):
        strict = eng in self.strict
        if strict and not (dma or cc):
            signal = True
        amt = 16 if dma else (1 if (signal or cc) else 0)
        sig_key = "cc" if cc else eng
        dl = []
        deps = list(deps)
        if strict and self.last_ev[eng] is not None:
            deps.append(self.last_ev[eng])
        for d in deps:
            if d is None:
                continue
            de, dt, da = d
            if de == eng and not (da or strict):
                continue
            dl.append((de, dt))
        self.steps[eng].append((fn, dl, amt if not cc else ("cc", amt), tag))
        if amt:
            self.counts[sig_key] += amt
            ev = (sig_key, self.counts[sig_key], dma or cc)
            if not cc:
                self.last_ev[eng] = ev
            return ev
        return None

    def wait_only(self, eng, deps, tag="wait"):
        dl = []
        for d in deps:
            if d is None:
                continue
            de, dt, da = d
            if de != eng or da:
                dl.append((de, dt))
        self.steps[eng].append((None, dl, 0, tag))

    def emit(self, block, sems):
        for eng_name in self.ENGINES:
            steps = self.steps[eng_name]
            if not steps:
                continue

            def body(eng, steps=steps, eng_name=eng_name):
                waited = {}
                for fn, deps, amt, tag in steps:
                    for (dep_eng, target) in deps:
                        if waited.get(dep_eng, 0) < target:
                            eng.wait_ge(sems[dep_eng], target)
                            waited[dep_eng] = target
                    if fn is None:
                        continue
                    inst = fn(eng)
                    if amt:
                        if isinstance(amt, tuple):
                            inst.then_inc(sems["cc"], amt[1])
                        else:
                            inst.then_inc(sems[eng_name], amt)

            getattr(block, eng_name)(body)


def build(upto="full", dbg=False, strict=False):
    nc = bass.Bass("TRN2", target_bir_lowering=False, debug=False)

    x_in = nc.dram_tensor("x_in", [128, NT * DM], F32, kind="ExternalInput").ap()
    wqkv_t = nc.dram_tensor("wqkv_t", [KC, 128, 3 * DM], F32, kind="ExternalInput").ap()
    wproj_t = nc.dram_tensor("wproj_t", [128, KC * DM], F32, kind="ExternalInput").ap()
    biasrow = nc.dram_tensor("biasrow", [1, DM], F32, kind="ExternalInput").ap()
    gate_in = nc.dram_tensor("gate_in", [128, KC * NE], F32, kind="ExternalInput").ap()
    wgu_r = nc.dram_tensor("wgu_r", [NE, 4, 128, 4096], F32, kind="ExternalInput").ap()
    wd_r = nc.dram_tensor("wd_r", [NE, 128, KC * DM], F32, kind="ExternalInput").ap()
    sim_in = nc.dram_tensor("sim_in", [NE, NE], F32, kind="ExternalInput").ap()
    cst_in = nc.dram_tensor("cst_in", [128, C_NCOLS], F32, kind="ExternalInput").ap()
    ident_in = nc.dram_tensor("ident_in", [128, 128], F32, kind="ExternalInput").ap()
    onesrow_in = nc.dram_tensor("onesrow_in", [1, 128], F32, kind="ExternalInput").ap()

    y_out = nc.dram_tensor("y", [TOK, DM], F32, kind="ExternalOutput").ap()
    if dbg:
        dbg_xnT = nc.dram_tensor("dbg_xnT", [128, 16384], BF16, kind="ExternalOutput").ap()
        dbg_ccs = nc.dram_tensor("dbg_ccs", [128, 1040], F32, kind="ExternalOutput").ap()
        dbg_ctxall = nc.dram_tensor("dbg_ctxall", [128, 1040], F32, kind="ExternalOutput").ap()
        dbg_ctxbd = nc.dram_tensor("dbg_ctxbd", [128, 1040], BF16, kind="ExternalOutput").ap()
        dbg_att = nc.dram_tensor("dbg_att", [128, 2048], BF16, kind="ExternalOutput").ap()
        dbg_z = nc.dram_tensor("dbg_z", [128, 256], F32, kind="ExternalOutput").ap()
        dbg_qT = nc.dram_tensor("dbg_qT", [128, 16384], BF16, kind="ExternalOutput").ap()
        dbg_kv = nc.dram_tensor("dbg_kv", [128, 4128], BF16, kind="ExternalOutput").ap()

    cc_ctx_in = nc.dram_tensor("cc_ctx_in", [128, 1040], F32).ap()
    cc_ctx_out = nc.dram_tensor("cc_ctx_out", [128, 1040], F32).ap()
    cc_cnt_in = nc.dram_tensor("cc_cnt_in", [1, NE], F32).ap()
    cc_cnt_out = nc.dram_tensor("cc_cnt_out", [1, NE], F32).ap()

    p = Prog(strict=strict)

    with ExitStack() as ectx:
        sb = lambda name, shape, dt: ectx.enter_context(nc.sbuf_tensor(name, shape, dt))
        psalloc = lambda name, shape, dt: ectx.enter_context(nc.psum_tensor(name, shape, dt))

        x_sb = sb("x_sb", [128, NT * DM], F32)
        rA = sb("rA", [128, 24576], BF16)
        rB = sb("rB", [128, 16384], BF16)
        xnT = sb("xnT", [128, 16384], BF16)
        kat = sb("kat", [128, 2048], BF16)
        vat = sb("vat", [128, 2080], BF16)
        xn_buf = sb("xn_buf", [128, 2048], BF16)
        ccs = sb("ccs", [128, 1040], F32)
        ctx_all = sb("ctx_all", [128, 1040], F32)
        ctxbd = sb("ctxbd", [128, NP * 128], BF16)
        ksb = sb("ksb", [128, NP * 16], BF16)
        cst = sb("cst", [128, C_NCOLS], F32)
        ident_b = sb("ident_b", [128, 128], BF16)
        onesrow_f = sb("onesrow_f", [1, 128], F32)
        onesrow_b = sb("onesrow_b", [1, 128], BF16)
        onescol_b = sb("onescol_b", [128, 1], BF16)
        biasrow_b = sb("biasrow_b", [1, DM], BF16)
        gate_sb = sb("gate_sb", [128, KC * NE], BF16)
        sim_sb = sb("sim_sb", [NE, NE], F32)
        ident8 = sb("ident8", [NE, NE], F32)
        mv = sb("mv", [128, 16], F32)
        lg = sb("lg", [128, NE], F32)
        m8 = sb("m8", [128, NE], F32)
        idxu = sb("idxu", [128, NE], U32)
        idxf = sb("idxf", [128, NE], F32)
        i12a = sb("i12a", [128, 2 * NT], F32)
        da = sb("da", [128, NT], F32)
        w1a = sb("w1a", [128, NT], F32)
        w2a = sb("w2a", [128, NT], F32)
        oh_scr = sb("oh_scr", [128, NT * NE], F32)
        oh2_scr = sb("oh2_scr", [128, NT * NE], F32)
        r1a = sb("r1a", [128, NT], F32)
        r2a = sb("r2a", [128, NT], F32)
        dense = sb("dense", [128, NT * NE], F32)
        ohbf = sb("ohbf", [128, NE], BF16)
        cnt_sb = sb("cnt_sb", [1, NE], F32)
        cnt_row = sb("cnt_row", [1, NE], F32)
        cnt8 = sb("cnt8", [NE, 8], F32)
        prow = sb("prow", [1, NE], F32)
        pmb = sb("pmb", [NE, NE], F32)
        msim = sb("msim", [NE, NE], F32)
        b8 = sb("b8", [NE, NE], F32)
        bidxu = sb("bidxu", [NE, NE], U32)
        bidxf = sb("bidxf", [NE, NE], F32)
        emrow = sb("emrow", [1, NE], F32)
        emb = sb("emb", [128, NE], F32)
        z2 = sb("z2", [128, 16], F32)
        zr = sb("zr", [128, 16], F32)

        psA = [psalloc(f"psA{i}", [128, 512], F32) for i in range(7)]
        psB16 = psalloc("psB16", [128, 1024], BF16)
        psBf32 = psB16.ap().bitcast(F32)

        sems = {e: ectx.enter_context(nc.semaphore(f"s_{e}")) for e in list(Prog.ENGINES) + ["cc"]}
        block = ectx.enter_context(nc.Block())

        ev_cst = p.step("sync", lambda e: e.dma_start(out=cst[:], in_=cst_in), dma=True, tag="cst")
        ev_simld = p.step("sync", lambda e: e.dma_start(out=sim_sb[:], in_=sim_in), dma=True)
        ev_i8 = p.step("sync", lambda e: e.dma_start(out=ident8[:], in_=ident_in[0:NE, 0:NE]), dma=True)
        ev_orf = p.step("sync", lambda e: e.dma_start(out=onesrow_f[:], in_=onesrow_in), dma=True)
        ev_x = []
        for g in range(4):
            ev = p.step(
                "sync", lambda e, g=g: e.dma_start(out=x_sb[:, g * 4 * DM:(g + 1) * 4 * DM],
                                                   in_=x_in[:, g * 4 * DM:(g + 1) * 4 * DM]),
                dma=True, tag=f"x{g}")
            ev_x += [ev] * 4

        ev_idb = p.step("gpsimd", lambda e: e.dma_start(out=ident_b[:], in_=ident_in), dma=True)
        ev_orb = p.step("gpsimd", lambda e: e.dma_start(out=onesrow_b[:], in_=onesrow_in), dma=True)
        ev_ocb = p.step("gpsimd", lambda e: e.dma_start(out=onescol_b[:], in_=cst_in[:, C_ONE:C_ONE + 1],
                                                        allow_slow_non_contiguous=True), dma=True)
        ev_bias = p.step("gpsimd", lambda e: e.dma_start(out=biasrow_b[:], in_=biasrow), dma=True)
        ev_gate = p.step("gpsimd", lambda e: e.dma_start(out=gate_sb[:], in_=gate_in), dma=True)
        ev_wqkv = []
        for kc in range(KC):
            ev_wqkv.append(p.step(
                "gpsimd", lambda e, kc=kc: e.dma_start(out=rA[:, kc * 3072:(kc + 1) * 3072],
                                                       in_=wqkv_t[kc]),
                dma=True, tag=f"wqkv{kc}"))

        last_psB = [None]

        def emit_ln(t, deps_in, war_dep, tag):
            p.step("vector", lambda e, t=t: e.bn_stats(out=mv[:, 0:6], in_=x_sb[:, t * DM:t * DM + 512]),
                   deps=list(deps_in) + [war_dep, ev_cst], tag=f"bn{tag}")
            p.step("vector", lambda e, t=t: e.bn_stats(out=mv[:, 6:12], in_=x_sb[:, t * DM + 512:(t + 1) * DM]))
            ev_a = p.step("vector", lambda e: e.bn_aggr(out=mv[:, 12:14], in_=mv[:, 0:12]), signal=True)
            ev_q = p.step("scalar", lambda e: e.activation(out=mv[:, 14:15], in_=mv[:, 13:14],
                                                           func=AF.Sqrt, bias=cst[:, C_LNEPS:C_LNEPS + 1]),
                          deps=[ev_a], signal=True, tag=f"sqrt{tag}")
            p.step("vector", lambda e: e.reciprocal(out=mv[:, 15:16], in_=mv[:, 14:15]), deps=[ev_q])
            return p.step("vector", lambda e, t=t: e.tensor_scalar(
                out=xn_buf[:, (t % 2) * DM:(t % 2 + 1) * DM], in0=x_sb[:, t * DM:(t + 1) * DM],
                scalar1=mv[:, 12:13], scalar2=mv[:, 15:16],
                op0=ALU.subtract, op1=ALU.mult), signal=True, tag=f"ts{tag}")

        def emit_transposes(t, ev_ts, tag):
            ev_tr = None
            for kc in range(KC):
                d = [ev_ts, ev_idb, last_psB[0]] if kc == 0 else []
                ev_tr = p.step("tensor", lambda e, t=t, kc=kc: e.transpose(
                    psB16[:, kc * 128:(kc + 1) * 128],
                    in_=xn_buf[:, (t % 2) * DM + kc * 128:(t % 2) * DM + (kc + 1) * 128],
                    identity=ident_b[:]), deps=d, signal=(kc == KC - 1), tag=f"tr{tag}")
            ev_evac = p.step("scalar", lambda e, t=t: e.copy(
                out=xnT.ap().rearrange("p (k c) -> p k c", k=KC, c=2048)[:, :, t * 128:(t + 1) * 128],
                in_=psB16.ap().rearrange("p (k c) -> p k c", k=KC, c=128)),
                deps=[ev_tr], signal=True, tag=f"tre{tag}")
            last_psB[0] = ev_evac
            return ev_tr, ev_evac

        ev_xn1T = [None] * NT
        ev_tr1 = [None] * NT

        def emit_ln1(t):
            war = ev_tr1[t - 2] if t >= 2 else None
            ev_ts = emit_ln(t, [ev_x[t]], war, f"1.{t}")
            ev_tr1[t], ev_xn1T[t] = emit_transposes(t, ev_ts, f"1.{t}")

        ev_vones = []
        for par in range(2):
            ev_vones.append(p.step("vector", lambda e, par=par: e.memset(
                vat.ap()[:, par * 1040:par * 1040 + 1032].rearrange(
                    "p (pr c) -> p pr c", pr=NP, c=129)[:, :, 128:129], 1.0),
                signal=True, tag=f"vones{par}"))

        ctx_bank_off = lambda pr: (2 + pr // 2, (pr % 2) * 130)
        ev_kv_evac = {}
        ev_ctx_tile = [None] * NT
        emit_ln1(0)
        emit_ln1(1)
        for t in range(NT):
            for g in range(4):
                bank = psA[g % 2]
                ev = None
                for kc in range(KC):
                    d = [ev_wqkv[kc]]
                    if kc == 0:
                        if g >= 2:
                            d.append(ev_kv_evac.get((t, g - 2)))
                        elif t > 0:
                            d.append(ev_kv_evac.get((t - 1, g + 2)))
                        d.append(ev_xn1T[t])
                    ev = p.step("tensor", lambda e, t=t, g=g, kc=kc, bank=bank: e.matmul(
                        bank[:, 0:512],
                        lhsT=xnT[:, kc * 2048 + t * 128: kc * 2048 + (t + 1) * 128],
                        rhs=rA[:, kc * 3072 + DM + g * 512: kc * 3072 + DM + (g + 1) * 512],
                        start=(kc == 0), stop=(kc == KC - 1)),
                        deps=d, signal=(kc == KC - 1), tag=f"kv{t}.{g}")
                war_kv = ev_ctx_tile[t - 2] if t >= 2 else None
                if g < 2:
                    ev_kv_evac[(t, g)] = p.step("scalar", lambda e, t=t, g=g, bank=bank: e.activation(
                        out=kat[:, (t % 2) * 1024 + g * 512:(t % 2) * 1024 + (g + 1) * 512],
                        in_=bank[:, 0:512], func=AF.Relu),
                        deps=[ev, war_kv], signal=True, tag=f"kev{t}.{g}")
                else:
                    gg = g - 2
                    ev_kv_evac[(t, g)] = p.step("scalar", lambda e, t=t, gg=gg, bank=bank: e.copy(
                        out=vat.ap()[:, (t % 2) * 1040 + gg * 516:(t % 2) * 1040 + gg * 516 + 516].rearrange(
                            "p (pr c) -> p pr c", pr=4, c=129)[:, :, 0:128],
                        in_=bank.ap().rearrange("p (pr c) -> p pr c", pr=4, c=128)),
                        deps=[ev, war_kv, ev_vones[t % 2]], signal=True, tag=f"vev{t}.{gg}")
            ev = None
            for pr in range(NP):
                bi, off = ctx_bank_off(pr)
                d = [ev_kv_evac[(t, 0 if pr < 4 else 1)], ev_kv_evac[(t, 2 if pr < 4 else 3)]]
                ev = p.step("tensor", lambda e, t=t, pr=pr, bi=bi, off=off: e.matmul(
                    psA[bi][:, off:off + 129],
                    lhsT=kat[:, (t % 2) * 1024 + pr * 128:(t % 2) * 1024 + (pr + 1) * 128],
                    rhs=vat[:, (t % 2) * 1040 + pr * 129:(t % 2) * 1040 + (pr + 1) * 129],
                    start=(t == 0 and pr % 2 == 0), stop=(t == NT - 1 and pr % 2 == 1)),
                    deps=d, signal=(pr == NP - 1), tag=f"ctx{t}.{pr}")
            ev_ctx_tile[t] = ev
            if t + 2 < NT:
                emit_ln1(t + 2)

        if dbg:
            p.step("sync", lambda e: e.dma_start(out=dbg_xnT, in_=xnT[:]),
                   deps=[ev_xn1T[NT - 1]], dma=True, tag="dbgxnT")
            p.step("sync", lambda e: e.dma_start(out=dbg_kv[:, 0:2048], in_=kat[:]),
                   deps=[ev_ctx_tile[1]], dma=True, tag="dbgkv1")
            p.step("sync", lambda e: e.dma_start(out=dbg_kv[:, 2048:4128], in_=vat[:, 0:2080]),
                   deps=[ev_ctx_tile[1]], dma=True, tag="dbgkv2")

        ev_ccs0 = p.step("vector", lambda e: e.memset(ccs[:], 0.0), signal=True, tag="ccs0")
        ev_ctx_evac = []
        for b in range(4):
            p.step("vector", lambda e, b=b: e.tensor_copy(
                out=ccs[:, b * 260:b * 260 + 129], in_=psA[2 + b][:, 0:129]),
                deps=[ev_ctx_tile[NT - 1], ev_ccs0], tag=f"ctxevA{b}")
            ev_ctx_evac.append(p.step("vector", lambda e, b=b: e.tensor_copy(
                out=ccs[:, b * 260 + 130:b * 260 + 259], in_=psA[2 + b][:, 130:259]),
                deps=[ev_ctx_tile[NT - 1]], signal=True, tag=f"ctxevB{b}"))
        ev_cc1_out = p.step("gpsimd", lambda e: e.dma_start(out=cc_ctx_in, in_=ccs[:]),
                            deps=ev_ctx_evac, dma=True, tag="cc1out")
        ev_cc1 = p.step("gpsimd", lambda e: e.collective_compute(
            "AllReduce", ALU.add, replica_groups=[[0, 1], [2, 3], [4, 5], [6, 7]],
            ins=[cc_ctx_in.opt()], outs=[cc_ctx_out.opt()]),
            deps=[ev_cc1_out], cc=True, tag="AR1")
        ev_ctx_rb = p.step("gpsimd", lambda e: e.dma_start(out=ctx_all[:], in_=cc_ctx_out),
                           deps=[ev_cc1], dma=True, tag="cc1rb")
        ev_bd0 = p.step("vector", lambda e: e.memset(ctxbd[:], 0.0), signal=True)
        ev_ks0 = p.step("vector", lambda e: e.memset(ksb[:], 0.0), signal=True)
        ev_ctxbd = []
        ev_ksb = None
        for pr in range(NP):
            bi, off = ctx_bank_off(pr)
            src = (bi - 2) * 260 + off
            p.step("vector", lambda e, pr=pr, src=src: e.tensor_copy(
                out=ctxbd[0:64, pr * 128:pr * 128 + 64], in_=ctx_all[0:64, src:src + 64]),
                deps=[ev_ctx_rb, ev_bd0])
            ev_ctxbd.append(p.step("vector", lambda e, pr=pr, src=src: e.tensor_copy(
                out=ctxbd[64:128, pr * 128 + 64:pr * 128 + 128], in_=ctx_all[64:128, src + 64:src + 128]),
                signal=True, tag=f"ctxbd{pr}"))
            p.step("vector", lambda e, pr=pr, src=src: e.tensor_copy(
                out=ksb[0:64, pr * 16 + 2 * pr:pr * 16 + 2 * pr + 1],
                in_=ctx_all[0:64, src + 128:src + 129]),
                deps=[ev_ks0])
            ev_ksb = p.step("vector", lambda e, pr=pr, src=src: e.tensor_copy(
                out=ksb[64:128, pr * 16 + 2 * pr + 1:pr * 16 + 2 * pr + 2],
                in_=ctx_all[64:128, src + 128:src + 129]),
                signal=True, tag=f"ksb{pr}")

        if dbg:
            p.step("sync", lambda e: e.dma_start(out=dbg_ccs, in_=ccs[:]),
                   deps=[ev_cc1_out], dma=True, tag="dbgccs")
            p.step("sync", lambda e: e.dma_start(out=dbg_ctxall, in_=ctx_all[:]),
                   deps=[ev_ctx_rb], dma=True, tag="dbgctxall")
            p.step("sync", lambda e: e.dma_start(out=dbg_ctxbd[:, 0:1024], in_=ctxbd[:]),
                   deps=[ev_ctxbd[NP - 1]], dma=True, tag="dbgctxbd")

        ev_q_evac = {}
        for nch in range(4):
            for oc in range(KC):
                bank = psA[oc % 2]
                ev = None
                for kc in range(KC):
                    d = [ev_wqkv[kc]]
                    if kc == 0:
                        prev = ev_q_evac.get((nch, oc - 2))
                        if prev is None and nch > 0:
                            prev = ev_q_evac.get((nch - 1, oc + 6))
                        if prev is None:
                            prev = ev_kv_evac[(NT - 1, 2 + (oc % 2))]
                        d.append(prev)
                    ev = p.step("tensor", lambda e, nch=nch, oc=oc, kc=kc, bank=bank: e.matmul(
                        bank[:, 0:512],
                        lhsT=rA[:, kc * 3072 + oc * 128: kc * 3072 + (oc + 1) * 128],
                        rhs=xnT[:, kc * 2048 + nch * 512: kc * 2048 + (nch + 1) * 512],
                        start=(kc == 0), stop=(kc == KC - 1)),
                        deps=d, signal=(kc == KC - 1), tag=f"q{nch}.{oc}")
                ev_q_evac[(nch, oc)] = p.step("scalar", lambda e, nch=nch, oc=oc, bank=bank: e.activation(
                    out=rB[:, oc * 2048 + nch * 512: oc * 2048 + (nch + 1) * 512],
                    in_=bank[:, 0:512], func=AF.Relu),
                    deps=[ev], signal=True, tag=f"qev{nch}.{oc}")
        ev_q_mm_done = ev

        ev_wproj = p.step("gpsimd", lambda e: e.dma_start(
            out=rA[:, 0:8192], in_=wproj_t),
            deps=[ev_q_mm_done], dma=True, tag="wproj")

        ev_zev = [None] * 4
        for ch in range(4):
            ev_dmm2 = None
            for pr in range(NP):
                d = [ev_ksb, ev_q_evac[(ch, pr)]]
                if pr == 0 and ch > 0:
                    d.append(ev_zev[ch - 1])
                ev_dmm2 = p.step("tensor", lambda e, ch=ch, pr=pr: e.matmul(
                    psA[6][0:16, 0:512],
                    lhsT=ksb[:, pr * 16:(pr + 1) * 16],
                    rhs=rB[:, pr * 2048 + ch * 512: pr * 2048 + (ch + 1) * 512],
                    start=(pr == 0), stop=(pr == NP - 1)),
                    deps=d, signal=(pr == NP - 1), tag=f"den{ch}.{pr}")
            ev_zev[ch] = p.step("scalar", lambda e, ch=ch: e.activation(
                out=kat[0:16, ch * 512:(ch + 1) * 512], in_=psA[6][0:16, 0:512],
                func=AF.Copy, bias=Z_EPS),
                deps=[ev_dmm2, ev_ctx_tile[NT - 1] if ch == 0 else None],
                signal=True, tag=f"zev{ch}")
        def _zrec(e):
            with nc.allow_low_precision(reason="z=1/(den+eps): den carries bf16 "
                                        "matmul-input rounding already; bar has 3x margin"):
                return e.reciprocal(out=kat[32:48, 0:2048], in_=kat[0:16, 0:2048])
        ev_zrec = p.step("vector", _zrec, deps=ev_zev, signal=True, tag="zrec")
        ev_ztr = None
        for t in range(NT):
            d = [ev_zrec, ev_idb]
            if t == 0:
                d.append(last_psB[0])
            ev_ztr = p.step("tensor", lambda e, t=t: e.transpose(
                psB16[:, t * 16:(t + 1) * 16],
                in_=kat[32:48, t * 128:(t + 1) * 128],
                identity=ident_b[32:48, 32:48]), deps=d, signal=(t == NT - 1), tag=f"ztr{t}")
        ev_ztok = p.step("vector", lambda e: e.tensor_copy(
            out=ccs[:, 0:256], in_=psB16[:, 0:256]),
            deps=[ev_ztr, ev_cc1_out], signal=True, tag="ztok")
        last_psB[0] = ev_ztok

        ev_att_tt = {}
        ev_att_pair = {}
        ev_a7ev = [None] * NT
        ev_projmm = [None] * NT
        ev_resid = {}

        def emit_att(t):
            b0 = 2 + (t % 2) * 2
            ev_sig = {}
            for pr in range(NP):
                b = b0 + pr // 4
                qq = pr // 4
                d = [ev_ctxbd[pr], ev_q_evac[(t // 4, pr)]]
                if t < 2:
                    d.append(ev_ctx_evac[b - 2])
                else:
                    d.append(ev_att_tt[(t - 2, qq)])
                ev_mm = p.step("tensor", lambda e, t=t, pr=pr, b=b: e.matmul(
                    psA[b][:, (pr % 4) * 128:(pr % 4) * 128 + 128],
                    lhsT=rB[:, pr * 2048 + t * 128: pr * 2048 + (t + 1) * 128],
                    rhs=ctxbd[:, pr * 128:(pr + 1) * 128],
                    start=True, stop=True),
                    deps=d, signal=(pr % 4 == 3), tag=f"att{t}.{pr}")
                if pr % 4 == 3:
                    ev_sig[qq] = ev_mm
            if t == NT - 1:
                for pp in range(4):
                    ev_att_pair[pp] = ev_sig[0]
                    ev_att_pair[4 + pp] = ev_sig[1]
            war_kat = ev_a7ev[t - 2] if t >= 2 else ev_ctx_tile[NT - 1]
            for qq in range(2):
                b = b0 + qq
                ev_att_tt[(t, qq)] = p.step("vector", lambda e, t=t, b=b, qq=qq: e.tensor_tensor(
                    out=kat.ap()[:, (t % 2) * 1024 + qq * 512:(t % 2) * 1024 + (qq + 1) * 512].rearrange(
                        "p (pr h c) -> p pr h c", pr=4, h=2, c=64),
                    in0=psA[b].ap()[:, 0:512].rearrange("p (pr h c) -> p pr h c",
                                                        pr=4, h=2, c=64),
                    in1=ccs.ap()[:, t * 16 + qq * 8: t * 16 + (qq + 1) * 8]
                        .rearrange("p (a c) -> p a c", a=4, c=2)
                        .unsqueeze(3).broadcast_to((128, 4, 2, 64)),
                    op=ALU.mult),
                    deps=[ev_sig[qq], war_kat, ev_ztok], signal=True, tag=f"attTT{t}.{qq}")

        def emit_proj(t):
            ev_tr = None
            for oc in range(KC):
                d = []
                if oc == 0:
                    d = [ev_att_tt[(t, 0)], ev_att_tt[(t, 1)], last_psB[0]]
                ev_tr = p.step("tensor", lambda e, t=t, oc=oc: e.transpose(
                    psB16[:, oc * 128:(oc + 1) * 128],
                    in_=kat[:, (t % 2) * 1024 + oc * 128:(t % 2) * 1024 + (oc + 1) * 128],
                    identity=ident_b[:]), deps=d, signal=(oc == KC - 1), tag=f"a7tr{t}")
            war_vat = ev_projmm[t - 2] if t >= 2 else ev_ctx_tile[NT - 1]
            ev_a7ev[t] = p.step("scalar", lambda e, t=t: e.copy(
                out=vat[:, (t % 2) * 1040:(t % 2) * 1040 + 1024], in_=psB16[:, 0:1024]),
                deps=[ev_tr, war_vat], signal=True, tag=f"a7ev{t}")
            last_psB[0] = ev_a7ev[t]
            ev_pm = {0: None, 1: None}
            for dc in range(KC):
                for hh in range(2):
                    d = [ev_a7ev[t], ev_wproj]
                    if dc == 0:
                        if t == 0:
                            d.append(ev_q_evac[(3, 6 + hh)])
                        else:
                            d.append(ev_resid[(t - 1, hh)])
                    p.step("tensor", lambda e, t=t, dc=dc, hh=hh: e.matmul(
                        psA[hh][:, 0:512],
                        lhsT=vat[:, (t % 2) * 1040 + dc * 128:(t % 2) * 1040 + (dc + 1) * 128],
                        rhs=rA[:, dc * 1024 + hh * 512: dc * 1024 + (hh + 1) * 512],
                        start=(dc == 0), stop=False),
                        deps=d, tag=f"proj{t}.{dc}.{hh}")
            for hh in range(2):
                ev_pm[hh] = p.step("tensor", lambda e, t=t, hh=hh: e.matmul(
                    psA[hh][:, 0:512],
                    lhsT=onesrow_b[:],
                    rhs=biasrow_b[:, hh * 512:(hh + 1) * 512],
                    start=False, stop=True),
                    deps=[ev_orb, ev_bias], signal=True, tag=f"bias{t}.{hh}")
            ev_projmm[t] = ev_pm[1]
            for hh in range(2):
                ev_resid[(t, hh)] = p.step("vector", lambda e, t=t, hh=hh: e.tensor_tensor(
                    out=x_sb[:, t * DM + hh * 512: t * DM + (hh + 1) * 512],
                    in0=psA[hh][:, 0:512],
                    in1=x_sb[:, t * DM + hh * 512: t * DM + (hh + 1) * 512],
                    op=ALU.add),
                    deps=[ev_pm[hh]], signal=True, tag=f"res{t}.{hh}")


        if dbg:
            p.step("sync", lambda e: e.dma_start(out=dbg_qT, in_=rB[:]),
                   deps=[ev_q_evac[(3, KC - 1)]], dma=True, tag="dbgqT")
            p.step("sync", lambda e: e.dma_start(out=dbg_att, in_=kat[:]),
                   deps=[ev_att_tt[(1, q)] for q in range(2)] + [ev_att_tt[(0, q)] for q in range(2)],
                   dma=True, tag="dbgatt")
            p.step("sync", lambda e: e.dma_start(out=dbg_z, in_=ccs[:, 0:256]),
                   deps=[ev_ztok], dma=True, tag="dbgz")

        if upto == "attn":
            ev_store = []
            for t in range(NT):
                ev_store.append(p.step("sync", lambda e, t=t: e.dma_start(
                    out=y_out[t * TPB:(t + 1) * TPB, :], in_=x_sb[:, t * DM:(t + 1) * DM]),
                    deps=[ev_resid[(t, 0)], ev_resid[(t, 1)]], dma=True, tag=f"st{t}"))
            p.wait_only("sync", ev_store, tag="final")
            p.emit(block, sems)
            return nc

        wd_col = lambda e_: 16384 if e_ % 2 == 0 else 8192
        wd_buf = lambda e_: rA if e_ % 2 == 0 else rB

        def emit_wd_dma(e_, deps):
            return p.step("gpsimd", lambda e, e_=e_: e.dma_start(
                out=wd_buf(e_)[:, wd_col(e_):wd_col(e_) + 8192],
                in_=wd_r[e_]),
                deps=deps, dma=True, tag=f"wd{e_}")

        def emit_wgu_dma(e_, g, deps):
            return p.step("gpsimd", lambda e, e_=e_, g=g: e.dma_start(
                out=rB[:, (g % 2) * 4096:(g % 2) * 4096 + 4096],
                in_=wgu_r[e_, g]),
                deps=deps, dma=True, tag=f"wgu{e_}.{g}")

        ev_wd = [None] * (NE + 1)
        ev_wgu = {}
        ev_wd[0] = emit_wd_dma(0, [ev_q_mm_done, ev_wproj])

        def emit_ln2_logits(t):
            war = ev_tr2[t - 2] if t >= 2 else None
            ev_ts = emit_ln(t, [ev_resid[(t, 0)], ev_resid[(t, 1)]], war, f"2.{t}")
            ev_tr2[t], ev_xn2T[t] = emit_transposes(t, ev_ts, f"2.{t}")
            ev_lmm = None
            for kc in range(KC):
                d = [ev_gate, ev_xn2T[t]]
                if kc == 0 and t > 0:
                    d.append(ev_cnt_add[t - 1])
                ev_lmm = p.step("tensor", lambda e, t=t, kc=kc: e.matmul(
                    psA[6][:, 0:8],
                    lhsT=xnT[:, kc * 2048 + t * 128: kc * 2048 + (t + 1) * 128],
                    rhs=gate_sb[:, kc * 8:(kc + 1) * 8],
                    start=(kc == 0), stop=(kc == KC - 1)),
                    deps=d, signal=(kc == KC - 1), tag=f"lg{t}.{kc}")
            ev_lg_evac[t] = p.step("vector", lambda e: e.tensor_copy(out=lg[:], in_=psA[6][:, 0:8]),
                                   deps=[ev_lmm], signal=True, tag=f"lgev{t}")
            p.step("vector", lambda e: e.max(out=m8[:], in_=lg[:]))
            p.step("vector", lambda e: e.max_index(out=idxu[:], in_max=m8[:], in_values=lg[:]))
            p.step("vector", lambda e: e.tensor_copy(out=idxf[:], in_=idxu[:]))
            p.step("vector", lambda e, t=t: e.tensor_copy(out=i12a[:, 2 * t:2 * t + 2], in_=idxf[:, 0:2]))
            p.step("vector", lambda e, t=t: e.tensor_tensor(
                out=da[:, t:t + 1], in0=m8[:, 0:1], in1=m8[:, 1:2], op=ALU.subtract))
            ev_oh = p.step("vector", lambda e: e.tensor_scalar(
                out=ohbf[:], in0=cst[:, C_IOTA0:C_IOTA0 + 8], scalar1=idxf[:, 0:1],
                scalar2=None, op0=ALU.is_equal), signal=True, tag=f"oh{t}")
            ev_cmm = p.step("tensor", lambda e: e.matmul(
                psA[6][0:1, 16:24], lhsT=onescol_b[:], rhs=ohbf[:],
                start=True, stop=True),
                deps=[ev_oh, ev_lg_evac[t], ev_ocb], signal=True, tag=f"cmm{t}")
            ev_cnt_add[t] = p.step("vector", lambda e: e.tensor_tensor(
                out=cnt_sb[:], in0=psA[6][0:1, 16:24], in1=cnt_sb[:], op=ALU.add),
                deps=[ev_cmm, ev_cnt0], signal=True, tag=f"cadd{t}")

        ev_xn2T = [None] * NT
        ev_cnt_add = [None] * NT
        ev_lg_evac = [None] * NT
        ev_tr2 = [None] * NT
        ev_cnt0 = p.step("vector", lambda e: e.memset(cnt_sb[:], 0.0), signal=True)

        for t in range(NT + 4):
            if t < NT:
                emit_att(t)
            if 1 <= t <= NT:
                emit_proj(t - 1)
            if 3 <= t <= NT + 2:
                emit_ln2_logits(t - 3)

        ev_wgu[(0, 0)] = emit_wgu_dma(0, 0, [ev_att_pair[0], ev_att_pair[1]])
        ev_wgu[(0, 1)] = emit_wgu_dma(0, 1, [ev_att_pair[2], ev_att_pair[3]])
        ev_wd[1] = emit_wd_dma(1, [ev_att_pair[4], ev_att_pair[5], ev_att_pair[6],
                                   ev_att_pair[7]])

        ev_cnt_out = p.step("gpsimd", lambda e: e.dma_start(out=cc_cnt_in, in_=cnt_sb[:]),
                            deps=[ev_cnt_add[NT - 1]], dma=True, tag="cc2out")
        ev_cc2 = p.step("gpsimd", lambda e: e.collective_compute(
            "AllReduce", ALU.add, replica_groups=[list(range(N_CORES))],
            ins=[cc_cnt_in.opt()], outs=[cc_cnt_out.opt()]),
            deps=[ev_cnt_out], cc=True, tag="AR2")
        ev_cnt_rb = p.step("gpsimd", lambda e: e.dma_start(out=cnt_row[:], in_=cc_cnt_out),
                           deps=[ev_cc2], dma=True, tag="cc2rb")

        routing_ev = {}

        def emit_routing(war_psA6):
            ev_csum = p.step("tensor", lambda e: e.matmul(
                psA[6][0:NE, 24:25], lhsT=cnt_row[:], rhs=onesrow_f[0:1, 0:1],
                start=True, stop=True),
                deps=[ev_cnt_rb, ev_cnt_add[NT - 1], ev_orf, war_psA6],
                signal=True, tag="csum")
            ev_c8 = p.step("vector", lambda e: e.tensor_copy(out=cnt8[:, 0:1], in_=psA[6][0:NE, 24:25]),
                           deps=[ev_csum], signal=True)
            p.step("vector", lambda e: e.tensor_scalar(
                out=cnt8[:, 1:2], in0=cnt8[:, 0:1], scalar1=0.0, scalar2=None, op0=ALU.is_gt))
            ev_prim = p.step("vector", lambda e: e.tensor_scalar(
                out=cnt8[:, 4:5], in0=cnt8[:, 1:2], scalar1=1.0, scalar2=-1.0,
                op0=ALU.subtract, op1=ALU.mult), signal=True, tag="nprim")
            ev_prmm = p.step("tensor", lambda e: e.matmul(
                psA[6][0:1, 32:40], lhsT=cnt8[0:NE, 1:2], rhs=ident8[:],
                start=True, stop=True), deps=[ev_prim, ev_i8], signal=True, tag="prmm")
            ev_prow = p.step("vector", lambda e: e.tensor_copy(out=prow[:], in_=psA[6][0:1, 32:40]),
                             deps=[ev_prmm], signal=True)
            ev_pmbm = p.step("tensor", lambda e: e.matmul(
                psA[6][0:NE, 40:48], lhsT=onesrow_f[0:1, 0:NE], rhs=prow[:],
                start=True, stop=True), deps=[ev_prow, ev_orf], signal=True, tag="pmbm")
            p.step("vector", lambda e: e.tensor_copy(out=pmb[:], in_=psA[6][0:NE, 40:48]),
                   deps=[ev_pmbm])
            p.step("vector", lambda e: e.scalar_tensor_tensor(
                out=msim[:], in0=pmb[:], scalar=1e9, in1=sim_sb[:], op0=ALU.mult, op1=ALU.add),
                deps=[ev_simld])
            p.step("vector", lambda e: e.tensor_scalar(
                out=msim[:], in0=msim[:], scalar1=1e9, scalar2=None, op0=ALU.subtract))
            p.step("vector", lambda e: e.max(out=b8[:], in_=msim[:]))
            p.step("vector", lambda e: e.max_index(out=bidxu[:], in_max=b8[:], in_values=msim[:]))
            p.step("vector", lambda e: e.tensor_copy(out=bidxf[:], in_=bidxu[:]))
            p.step("vector", lambda e: e.tensor_scalar(
                out=cnt8[:, 5:6], in0=b8[0:NE, 0:1], scalar1=cst[0:NE, C_THR:C_THR + 1],
                scalar2=None, op0=ALU.is_ge))
            p.step("vector", lambda e: e.tensor_tensor(
                out=cnt8[:, 2:3], in0=cnt8[:, 5:6], in1=cnt8[:, 4:5], op=ALU.mult))
            p.step("vector", lambda e: e.tensor_scalar(
                out=cnt8[:, 6:7], in0=cnt8[:, 2:3], scalar1=1.0, scalar2=-1.0,
                op0=ALU.subtract, op1=ALU.mult))
            p.step("vector", lambda e: e.tensor_tensor(
                out=cnt8[:, 7:8], in0=bidxf[0:NE, 0:1], in1=cnt8[:, 2:3], op=ALU.mult))
            p.step("vector", lambda e: e.tensor_tensor(
                out=cnt8[:, 6:7], in0=cst[0:NE, C_PIDX:C_PIDX + 1], in1=cnt8[:, 6:7], op=ALU.mult))
            ev_em = p.step("vector", lambda e: e.tensor_tensor(
                out=cnt8[:, 3:4], in0=cnt8[:, 7:8], in1=cnt8[:, 6:7], op=ALU.add),
                signal=True, tag="em")
            ev_emm = p.step("tensor", lambda e: e.matmul(
                psA[6][0:1, 48:56], lhsT=cnt8[0:NE, 3:4], rhs=ident8[:],
                start=True, stop=True), deps=[ev_em], signal=True, tag="emm")
            ev_emrow = p.step("vector", lambda e: e.tensor_copy(out=emrow[:], in_=psA[6][0:1, 48:56]),
                              deps=[ev_emm], signal=True)
            ev_embm = p.step("tensor", lambda e: e.matmul(
                psA[6][0:128, 56:64], lhsT=onesrow_f[:], rhs=emrow[:],
                start=True, stop=True), deps=[ev_emrow], signal=True, tag="embm")
            ev_emb = p.step("vector", lambda e: e.tensor_copy(out=emb[:], in_=psA[6][0:128, 56:64]),
                            deps=[ev_embm], signal=True)

            iota_b = cst.ap()[:, C_IOTA0:C_IOTA0 + 8].unsqueeze(1).broadcast_to((128, NT, NE))
            i1_b = i12a.ap().rearrange("p (t k) -> p t k", t=NT, k=2)[:, :, 0:1].broadcast_to((128, NT, NE))
            i2_b = i12a.ap().rearrange("p (t k) -> p t k", t=NT, k=2)[:, :, 1:2].broadcast_to((128, NT, NE))
            oh3 = oh_scr.ap().rearrange("p (t e) -> p t e", t=NT, e=NE)
            oh23 = oh2_scr.ap().rearrange("p (t e) -> p t e", t=NT, e=NE)
            dense3 = dense.ap().rearrange("p (t e) -> p t e", t=NT, e=NE)
            emb_b = emb.ap().unsqueeze(1).broadcast_to((128, NT, NE))
            p.step("vector", lambda e: e.tensor_tensor(out=oh3, in0=iota_b, in1=i1_b, op=ALU.is_equal),
                   deps=[ev_emb])
            p.step("vector", lambda e: e.tensor_tensor(out=oh3, in0=oh3, in1=emb_b, op=ALU.mult))
            p.step("vector", lambda e: e.tensor_reduce(out=r1a[:], in_=oh3, axis=AX.X, op=ALU.add))
            p.step("vector", lambda e: e.tensor_tensor(out=oh23, in0=iota_b, in1=i2_b, op=ALU.is_equal))
            p.step("vector", lambda e: e.tensor_tensor(out=oh23, in0=oh23, in1=emb_b, op=ALU.mult))
            p.step("vector", lambda e: e.tensor_reduce(out=r2a[:], in_=oh23, axis=AX.X, op=ALU.add))
            ev_da_done = p.step("vector", lambda e: e.tensor_copy(out=w2a[:], in_=da[:]), signal=True)
            ev_sig = p.step("scalar", lambda e: e.activation(out=w1a[:], in_=w2a[:], func=AF.Sigmoid),
                            deps=[ev_da_done], signal=True, tag="sigmoid")
            p.step("vector", lambda e: e.tensor_scalar(
                out=w2a[:], in0=w1a[:], scalar1=1.0, scalar2=-1.0, op0=ALU.subtract, op1=ALU.mult),
                deps=[ev_sig])
            r1_b = r1a.ap().unsqueeze(2).broadcast_to((128, NT, NE))
            r2_b = r2a.ap().unsqueeze(2).broadcast_to((128, NT, NE))
            w1_b = w1a.ap().unsqueeze(2).broadcast_to((128, NT, NE))
            w2_b = w2a.ap().unsqueeze(2).broadcast_to((128, NT, NE))
            p.step("vector", lambda e: e.tensor_tensor(out=oh3, in0=iota_b, in1=r1_b, op=ALU.is_equal))
            p.step("vector", lambda e: e.tensor_tensor(out=dense3, in0=oh3, in1=w1_b, op=ALU.mult))
            p.step("vector", lambda e: e.tensor_tensor(out=oh23, in0=iota_b, in1=r2_b, op=ALU.is_equal))
            p.step("vector", lambda e: e.tensor_tensor(out=oh23, in0=oh23, in1=w2_b, op=ALU.mult))
            ev_dense = p.step("vector", lambda e: e.tensor_tensor(
                out=dense3, in0=dense3, in1=oh23, op=ALU.add), signal=True, tag="dense")
            routing_ev["emb"] = ev_emb
            routing_ev["dense"] = ev_dense


        def slot_ap(s, width=512):
            if s < 7:
                return psA[s][:, 0:width]
            return psBf32[:, 0:width]

        ev_gu_fc_last = {}
        ev_sE = {}
        ev_uE = {}
        ev_down_last = [None] * NE
        ev_stt = {}

        last_slot_epilogue = {
            0: ev_resid[(NT - 1, 0)],
            1: ev_resid[(NT - 1, 1)],
            2: ev_att_tt[(NT - 2, 0)],
            3: ev_att_tt[(NT - 2, 1)],
            4: ev_att_tt[(NT - 1, 0)],
            5: ev_att_tt[(NT - 1, 1)],
            6: ev_cnt_add[NT - 1],
            7: last_psB[0],
        }

        def emit_gu(e_):
            for fc in range(FCG):
                g = fc // 4
                ev_fc = None
                for kc in range(KC):
                    for nch in range(4):
                        s = (fc % 2) * 4 + nch
                        d = [ev_wgu[(e_, g)]]
                        if kc == 0:
                            d.append(last_slot_epilogue.get(s))
                            d.append(ev_xn2T[4 * nch + 3])
                        ev_fc = p.step("tensor", lambda e, fc=fc, kc=kc, nch=nch, s=s: e.matmul(
                            slot_ap(s),
                            lhsT=rB[:, (fc // 4 % 2) * 4096 + (fc % 4) * 1024 + kc * 128:
                                    (fc // 4 % 2) * 4096 + (fc % 4) * 1024 + (kc + 1) * 128],
                            rhs=xnT[:, kc * 2048 + nch * 512: kc * 2048 + (nch + 1) * 512],
                            start=(kc == 0), stop=(kc == KC - 1)),
                            deps=d, signal=(kc == KC - 1), tag=f"gu{e_}.{fc}.{nch}")
                        if kc == KC - 1:
                            if fc < 8:
                                dd = [ev_fc]
                                if e_ > 0:
                                    dd.append(ev_down_last[e_ - 1])
                                ev_sE[(e_, fc, nch)] = p.step("scalar",
                                    lambda e, fc=fc, nch=nch, s=s: e.activation(
                                        out=rA[:, fc * 2048 + nch * 512: fc * 2048 + (nch + 1) * 512],
                                        in_=slot_ap(s), func=AF.Silu),
                                    deps=dd, signal=True, tag=f"silu{e_}.{fc}.{nch}")
                                last_slot_epilogue[s] = ev_sE[(e_, fc, nch)]
                            else:
                                fcc = fc - 8
                                ev_uE[(e_, fcc, nch)] = p.step("vector",
                                    lambda e, fcc=fcc, nch=nch, s=s: e.tensor_tensor(
                                        out=rA[:, fcc * 2048 + nch * 512: fcc * 2048 + (nch + 1) * 512],
                                        in0=rA[:, fcc * 2048 + nch * 512: fcc * 2048 + (nch + 1) * 512],
                                        in1=slot_ap(s), op=ALU.mult),
                                    deps=[ev_fc, ev_sE[(e_, fcc, nch)]], signal=True,
                                    tag=f"umul{e_}.{fcc}.{nch}")
                                last_slot_epilogue[s] = ev_uE[(e_, fcc, nch)]
                ev_gu_fc_last[(e_, fc)] = ev_fc
                if fc == 3:
                    ev_wgu[(e_, 2)] = emit_wgu_dma(e_, 2, [ev_fc])
                elif fc == 7:
                    ev_wgu[(e_, 3)] = emit_wgu_dma(e_, 3, [ev_fc])
                elif fc == 11 and e_ + 1 < NE:
                    ev_wgu[(e_ + 1, 0)] = emit_wgu_dma(e_ + 1, 0, [ev_fc])
                elif fc == 15 and e_ + 1 < NE:
                    ev_wgu[(e_ + 1, 1)] = emit_wgu_dma(e_ + 1, 1, [ev_fc])

        def emit_down(e_):
            ev_dmm = None
            for t in range(NT):
                pA0, pA1 = (0, 1) if t % 2 == 0 else (2, 3)
                for fc in range(KC):
                    for hh in range(2):
                        d = [ev_wd[e_], ev_uE[(e_, fc, t // 4)]]
                        if fc == 0 and hh == 0:
                            if t < 2:
                                d += [last_slot_epilogue[pA0], last_slot_epilogue[pA1]]
                            else:
                                d += [ev_stt[(e_, t - 2, 0)], ev_stt[(e_, t - 2, 1)]]
                        ev_dmm = p.step("tensor",
                            lambda e, e_=e_, t=t, fc=fc, hh=hh, pA=(pA0 if hh == 0 else pA1): e.matmul(
                                psA[pA][:, 0:512],
                                lhsT=rA[:, fc * 2048 + t * 128: fc * 2048 + (t + 1) * 128],
                                rhs=wd_buf(e_)[:, wd_col(e_) + fc * 1024 + hh * 512:
                                               wd_col(e_) + fc * 1024 + (hh + 1) * 512],
                                start=(fc == 0), stop=(fc == KC - 1)),
                            deps=d, signal=(fc == KC - 1), tag=f"dn{e_}.{t}.{hh}")
                for hh in range(2):
                    pA = pA0 if hh == 0 else pA1
                    ev_stt[(e_, t, hh)] = p.step("vector",
                        lambda e, e_=e_, t=t, hh=hh, pA=pA: e.scalar_tensor_tensor(
                            out=x_sb[:, t * DM + hh * 512: t * DM + (hh + 1) * 512],
                            in0=psA[pA][:, 0:512],
                            scalar=dense[:, t * NE + e_: t * NE + e_ + 1],
                            in1=x_sb[:, t * DM + hh * 512: t * DM + (hh + 1) * 512],
                            op0=ALU.mult, op1=ALU.add),
                        deps=[ev_dmm, routing_ev["dense"]], signal=True, tag=f"stt{e_}.{t}.{hh}")
            ev_down_last[e_] = ev_dmm
            last_slot_epilogue[0] = ev_stt[(e_, NT - 2, 0)]
            last_slot_epilogue[1] = ev_stt[(e_, NT - 2, 1)]
            last_slot_epilogue[2] = ev_stt[(e_, NT - 1, 0)]
            last_slot_epilogue[3] = ev_stt[(e_, NT - 1, 1)]
            if e_ + 2 <= NE - 1:
                ev_wd[e_ + 2] = emit_wd_dma(e_ + 2, [ev_down_last[e_]])

        emit_gu(0)
        emit_routing(last_slot_epilogue[6])
        last_slot_epilogue[6] = routing_ev["emb"]
        emit_down(0)
        for e_ in range(1, NE):
            emit_gu(e_)
            emit_down(e_)

        ev_store = []
        for t in range(NT):
            ev_store.append(p.step("sync", lambda e, t=t: e.dma_start(
                out=y_out[t * TPB:(t + 1) * TPB, :], in_=x_sb[:, t * DM:(t + 1) * DM]),
                deps=[ev_stt[(NE - 1, t, 0)], ev_stt[(NE - 1, t, 1)]], dma=True, tag=f"st{t}"))
        p.wait_only("sync", ev_store, tag="final")

        p.emit(block, sems)

    return nc



def _prep_inputs(L0_attn_x, L0_attn_w_qkv, L0_attn_w_proj, L0_attn_b_proj,
                 L0_ffn_gate_weight, L0_ffn_gate_up_proj, L0_ffn_down_proj,
                 L0_ffn_similarity_matrix):
    x = np.ascontiguousarray(
        np.asarray(L0_attn_x, dtype=np.float32).reshape(N_CORES, NT, TPB, DM)
        .transpose(0, 2, 1, 3).reshape(N_CORES, TPB, NT * DM))
    wqkv = np.ascontiguousarray(np.asarray(L0_attn_w_qkv, dtype=np.float32).T).reshape(KC, 128, 3 * DM)
    wproj = np.ascontiguousarray(
        np.asarray(L0_attn_w_proj, dtype=np.float32).T.reshape(KC, 128, DM)
        .transpose(1, 0, 2).reshape(128, KC * DM))
    brow = np.ascontiguousarray(np.asarray(L0_attn_b_proj, dtype=np.float32).reshape(1, DM))
    gw = np.asarray(L0_ffn_gate_weight, dtype=np.float32)
    gate_in = np.ascontiguousarray(gw.T.reshape(KC, 128, NE).transpose(1, 0, 2).reshape(128, KC * NE))
    gu = np.asarray(L0_ffn_gate_up_proj, dtype=np.float32)
    wgu = np.ascontiguousarray(
        gu.reshape(NE, 4, 4, 128, KC, 128).transpose(0, 1, 5, 2, 4, 3)
        .reshape(NE, 4, 128, 4096))
    dn = np.asarray(L0_ffn_down_proj, dtype=np.float32)
    wd = np.ascontiguousarray(
        dn.transpose(0, 2, 1).reshape(NE, KC, 128, DM).transpose(0, 2, 1, 3)
        .reshape(NE, 128, KC * DM))
    sim = np.ascontiguousarray(np.asarray(L0_ffn_similarity_matrix, dtype=np.float32))
    ident = np.eye(128, dtype=np.float32)
    onesrow = np.ones((1, 128), dtype=np.float32)

    in_maps = []
    for c in range(N_CORES):
        cstv = np.zeros((128, C_NCOLS), dtype=np.float32)
        cstv[:, C_LNEPS] = LN_EPS
        cstv[:, C_ZEPS] = Z_EPS
        cstv[:, C_THR] = THR
        cstv[:, C_ONE] = 1.0
        cstv[:, C_IOTA0:C_IOTA0 + 8] = np.arange(8, dtype=np.float32)[None, :]
        cstv[:, C_PIDX] = np.arange(128, dtype=np.float32)
        b = c // 2
        cstv[:, C_CSEL0 + 2 * b] = 1.0
        cstv[:, C_CSEL0 + 2 * b + 1] = 1.0
        in_maps.append({
            "x_in": x[c],
            "wqkv_t": wqkv,
            "wproj_t": wproj,
            "biasrow": brow,
            "gate_in": gate_in,
            "wgu_r": wgu,
            "wd_r": wd,
            "sim_in": sim,
            "cst_in": cstv,
            "ident_in": ident,
            "onesrow_in": onesrow,
        })
    return in_maps


def kernel(L0_attn_x, L0_attn_w_qkv, L0_attn_w_proj, L0_attn_b_proj,
           L0_ffn_gate_weight, L0_ffn_gate_up_proj, L0_ffn_down_proj,
           L0_ffn_similarity_matrix, _upto="full",
           _strict=frozenset(("vector", "scalar"))):
    in_maps = _prep_inputs(L0_attn_x, L0_attn_w_qkv, L0_attn_w_proj, L0_attn_b_proj,
                           L0_ffn_gate_weight, L0_ffn_gate_up_proj, L0_ffn_down_proj,
                           L0_ffn_similarity_matrix)
    nc = build(upto=_upto, strict=_strict)
    res = run_bass_kernel_spmd(nc, in_maps, core_ids=list(range(N_CORES)))
    y = np.concatenate([np.asarray(r["y"], dtype=np.float32) for r in res.results], axis=0)
    return y.reshape(4, 4096, DM)


# version 09
# speedup vs baseline: 1.0393x; 1.0393x over previous
import numpy as np
import ml_dtypes
from contextlib import ExitStack

import concourse.bass as bass
import concourse.mybir as mybir
from concourse.bass_utils import run_bass_kernel_spmd

F32 = mybir.dt.float32
BF16 = mybir.dt.bfloat16
U32 = mybir.dt.uint32
AF = mybir.ActivationFunctionType
ALU = mybir.AluOpType
AX = mybir.AxisListType

N_CORES = 8
TPB = 128
NT = 16
TOK = TPB * NT
DM = 1024
KC = 8
NP = 8
NE = 8
FCG = 16
LN_EPS = 1e-5
Z_EPS = float(np.finfo(np.float32).eps)
THR = 0.7

C_LNEPS, C_ZEPS, C_THR, C_ONE, C_IOTA0, C_PIDX, C_CSEL0 = 0, 1, 2, 3, 4, 12, 13
C_NCOLS = 24


class Prog:
    ENGINES = ("tensor", "vector", "scalar", "gpsimd", "sync")

    def __init__(self, strict=frozenset()):
        self.steps = {e: [] for e in self.ENGINES}
        self.counts = {e: 0 for e in self.ENGINES}
        self.counts["cc"] = 0
        self.strict = frozenset(strict) if not isinstance(strict, bool) else (
            frozenset(self.ENGINES) if strict else frozenset())
        self.last_ev = {e: None for e in self.ENGINES}

    def step(self, eng, fn, deps=(), signal=False, dma=False, cc=False, tag=""):
        strict = eng in self.strict
        if strict and not (dma or cc):
            signal = True
        amt = 16 if dma else (1 if (signal or cc) else 0)
        sig_key = "cc" if cc else eng
        dl = []
        deps = list(deps)
        if strict and self.last_ev[eng] is not None:
            deps.append(self.last_ev[eng])
        for d in deps:
            if d is None:
                continue
            de, dt, da = d
            if de == eng and not (da or strict):
                continue
            dl.append((de, dt))
        self.steps[eng].append((fn, dl, amt if not cc else ("cc", amt), tag))
        if amt:
            self.counts[sig_key] += amt
            ev = (sig_key, self.counts[sig_key], dma or cc)
            if not cc:
                self.last_ev[eng] = ev
            return ev
        return None

    def wait_only(self, eng, deps, tag="wait"):
        dl = []
        for d in deps:
            if d is None:
                continue
            de, dt, da = d
            if de != eng or da:
                dl.append((de, dt))
        self.steps[eng].append((None, dl, 0, tag))

    def emit(self, block, sems):
        for eng_name in self.ENGINES:
            steps = self.steps[eng_name]
            if not steps:
                continue

            def body(eng, steps=steps, eng_name=eng_name):
                waited = {}
                for fn, deps, amt, tag in steps:
                    for (dep_eng, target) in deps:
                        if waited.get(dep_eng, 0) < target:
                            eng.wait_ge(sems[dep_eng], target)
                            waited[dep_eng] = target
                    if fn is None:
                        continue
                    inst = fn(eng)
                    if amt:
                        if isinstance(amt, tuple):
                            inst.then_inc(sems["cc"], amt[1])
                        else:
                            inst.then_inc(sems[eng_name], amt)

            getattr(block, eng_name)(body)


def build(upto="full", dbg=False, strict=False):
    nc = bass.Bass("TRN2", target_bir_lowering=False, debug=False)

    x_in = nc.dram_tensor("x_in", [128, NT * DM], F32, kind="ExternalInput").ap()
    wqkv_t = nc.dram_tensor("wqkv_t", [KC, 128, 3 * DM], F32, kind="ExternalInput").ap()
    wproj_t = nc.dram_tensor("wproj_t", [128, KC * DM], F32, kind="ExternalInput").ap()
    biasrow = nc.dram_tensor("biasrow", [1, DM], F32, kind="ExternalInput").ap()
    gate_in = nc.dram_tensor("gate_in", [128, KC * NE], F32, kind="ExternalInput").ap()
    wgu_r = nc.dram_tensor("wgu_r", [NE, 4, 128, 4096], F32, kind="ExternalInput").ap()
    wd_r = nc.dram_tensor("wd_r", [NE, 128, KC * DM], F32, kind="ExternalInput").ap()
    sim_in = nc.dram_tensor("sim_in", [NE, NE], F32, kind="ExternalInput").ap()
    cst_in = nc.dram_tensor("cst_in", [128, C_NCOLS], F32, kind="ExternalInput").ap()
    ident_in = nc.dram_tensor("ident_in", [128, 128], F32, kind="ExternalInput").ap()
    onesrow_in = nc.dram_tensor("onesrow_in", [1, 128], F32, kind="ExternalInput").ap()

    y_out = nc.dram_tensor("y", [TOK, DM], F32, kind="ExternalOutput").ap()
    if dbg:
        dbg_xnT = nc.dram_tensor("dbg_xnT", [128, 16384], BF16, kind="ExternalOutput").ap()
        dbg_ccs = nc.dram_tensor("dbg_ccs", [128, 1040], F32, kind="ExternalOutput").ap()
        dbg_ctxall = nc.dram_tensor("dbg_ctxall", [128, 1040], F32, kind="ExternalOutput").ap()
        dbg_ctxbd = nc.dram_tensor("dbg_ctxbd", [128, 1040], BF16, kind="ExternalOutput").ap()
        dbg_att = nc.dram_tensor("dbg_att", [128, 2048], BF16, kind="ExternalOutput").ap()
        dbg_z = nc.dram_tensor("dbg_z", [128, 256], F32, kind="ExternalOutput").ap()
        dbg_qT = nc.dram_tensor("dbg_qT", [128, 16384], BF16, kind="ExternalOutput").ap()
        dbg_kv = nc.dram_tensor("dbg_kv", [128, 4128], BF16, kind="ExternalOutput").ap()

    cc_ctx_in = nc.dram_tensor("cc_ctx_in", [128, 1040], F32).ap()
    cc_ctx_out = nc.dram_tensor("cc_ctx_out", [128, 1040], F32).ap()
    cc_cnt_in = nc.dram_tensor("cc_cnt_in", [1, NE], F32).ap()
    cc_cnt_out = nc.dram_tensor("cc_cnt_out", [1, NE], F32).ap()

    p = Prog(strict=strict)

    with ExitStack() as ectx:
        sb = lambda name, shape, dt: ectx.enter_context(nc.sbuf_tensor(name, shape, dt))
        psalloc = lambda name, shape, dt: ectx.enter_context(nc.psum_tensor(name, shape, dt))

        x_sb = sb("x_sb", [128, NT * DM], F32)
        rA = sb("rA", [128, 24576], BF16)
        rB = sb("rB", [128, 16384], BF16)
        xnT = sb("xnT", [128, 16384], BF16)
        kat = sb("kat", [128, 2048], BF16)
        vat = sb("vat", [128, 2080], BF16)
        xn_buf = sb("xn_buf", [128, 2048], BF16)
        ccs = sb("ccs", [128, 1040], F32)
        ctx_all = sb("ctx_all", [128, 1040], F32)
        ctxbd = sb("ctxbd", [128, NP * 128], BF16)
        ksb = sb("ksb", [128, NP * 16], BF16)
        cst = sb("cst", [128, C_NCOLS], F32)
        ident_b = sb("ident_b", [128, 128], BF16)
        onesrow_f = sb("onesrow_f", [1, 128], F32)
        onesrow_b = sb("onesrow_b", [1, 128], BF16)
        onescol_b = sb("onescol_b", [128, 1], BF16)
        biasrow_b = sb("biasrow_b", [1, DM], BF16)
        gate_sb = sb("gate_sb", [128, KC * NE], BF16)
        sim_sb = sb("sim_sb", [NE, NE], F32)
        ident8 = sb("ident8", [NE, NE], F32)
        mv = sb("mv", [128, 16], F32)
        lg = sb("lg", [128, NE], F32)
        m8 = sb("m8", [128, NE], F32)
        idxu = sb("idxu", [128, NE], U32)
        idxf = sb("idxf", [128, NE], F32)
        i12a = sb("i12a", [128, 2 * NT], F32)
        da = sb("da", [128, NT], F32)
        w1a = sb("w1a", [128, NT], F32)
        w2a = sb("w2a", [128, NT], F32)
        oh_scr = sb("oh_scr", [128, NT * NE], F32)
        oh2_scr = sb("oh2_scr", [128, NT * NE], F32)
        r1a = sb("r1a", [128, NT], F32)
        r2a = sb("r2a", [128, NT], F32)
        dense = sb("dense", [128, NT * NE], F32)
        ohbf = sb("ohbf", [128, NE], BF16)
        cnt_sb = sb("cnt_sb", [1, NE], F32)
        cnt_row = sb("cnt_row", [1, NE], F32)
        cnt8 = sb("cnt8", [NE, 8], F32)
        prow = sb("prow", [1, NE], F32)
        pmb = sb("pmb", [NE, NE], F32)
        msim = sb("msim", [NE, NE], F32)
        b8 = sb("b8", [NE, NE], F32)
        bidxu = sb("bidxu", [NE, NE], U32)
        bidxf = sb("bidxf", [NE, NE], F32)
        emrow = sb("emrow", [1, NE], F32)
        emb = sb("emb", [128, NE], F32)
        z2 = sb("z2", [128, 16], F32)
        zr = sb("zr", [128, 16], F32)

        psA = [psalloc(f"psA{i}", [128, 512], F32) for i in range(7)]
        psB16 = psalloc("psB16", [128, 1024], BF16)
        psBf32 = psB16.ap().bitcast(F32)

        sems = {e: ectx.enter_context(nc.semaphore(f"s_{e}")) for e in list(Prog.ENGINES) + ["cc"]}
        block = ectx.enter_context(nc.Block())

        ev_cst = p.step("sync", lambda e: e.dma_start(out=cst[:], in_=cst_in), dma=True, tag="cst")
        ev_simld = p.step("sync", lambda e: e.dma_start(out=sim_sb[:], in_=sim_in), dma=True)
        ev_i8 = p.step("sync", lambda e: e.dma_start(out=ident8[:], in_=ident_in[0:NE, 0:NE]), dma=True)
        ev_orf = p.step("sync", lambda e: e.dma_start(out=onesrow_f[:], in_=onesrow_in), dma=True)
        ev_x = []
        for g in range(4):
            ev = p.step(
                "sync", lambda e, g=g: e.dma_start(out=x_sb[:, g * 4 * DM:(g + 1) * 4 * DM],
                                                   in_=x_in[:, g * 4 * DM:(g + 1) * 4 * DM]),
                dma=True, tag=f"x{g}")
            ev_x += [ev] * 4

        ev_idb = p.step("gpsimd", lambda e: e.dma_start(out=ident_b[:], in_=ident_in), dma=True)
        ev_orb = p.step("gpsimd", lambda e: e.dma_start(out=onesrow_b[:], in_=onesrow_in), dma=True)
        ev_ocb = p.step("gpsimd", lambda e: e.dma_start(out=onescol_b[:], in_=cst_in[:, C_ONE:C_ONE + 1],
                                                        allow_slow_non_contiguous=True), dma=True)
        ev_bias = p.step("gpsimd", lambda e: e.dma_start(out=biasrow_b[:], in_=biasrow), dma=True)
        ev_gate = p.step("gpsimd", lambda e: e.dma_start(out=gate_sb[:], in_=gate_in), dma=True)
        ev_wqkv = []
        for kc in range(KC):
            ev_wqkv.append(p.step(
                "gpsimd", lambda e, kc=kc: e.dma_start(out=rA[:, kc * 3072:(kc + 1) * 3072],
                                                       in_=wqkv_t[kc]),
                dma=True, tag=f"wqkv{kc}"))

        last_psB = [None]

        def emit_ln(t, deps_in, war_dep, tag):
            p.step("vector", lambda e, t=t: e.bn_stats(out=mv[:, 0:6], in_=x_sb[:, t * DM:t * DM + 512]),
                   deps=list(deps_in) + [war_dep, ev_cst], tag=f"bn{tag}")
            p.step("vector", lambda e, t=t: e.bn_stats(out=mv[:, 6:12], in_=x_sb[:, t * DM + 512:(t + 1) * DM]))
            ev_a = p.step("vector", lambda e: e.bn_aggr(out=mv[:, 12:14], in_=mv[:, 0:12]), signal=True)
            ev_q = p.step("scalar", lambda e: e.activation(out=mv[:, 14:15], in_=mv[:, 13:14],
                                                           func=AF.Sqrt, bias=cst[:, C_LNEPS:C_LNEPS + 1]),
                          deps=[ev_a], signal=True, tag=f"sqrt{tag}")
            p.step("vector", lambda e: e.reciprocal(out=mv[:, 15:16], in_=mv[:, 14:15]), deps=[ev_q])
            return p.step("vector", lambda e, t=t: e.tensor_scalar(
                out=xn_buf[:, (t % 2) * DM:(t % 2 + 1) * DM], in0=x_sb[:, t * DM:(t + 1) * DM],
                scalar1=mv[:, 12:13], scalar2=mv[:, 15:16],
                op0=ALU.subtract, op1=ALU.mult), signal=True, tag=f"ts{tag}")

        def emit_transposes(t, ev_ts, tag):
            ev_tr = None
            for kc in range(KC):
                d = [ev_ts, ev_idb, last_psB[0]] if kc == 0 else []
                ev_tr = p.step("tensor", lambda e, t=t, kc=kc: e.transpose(
                    psB16[:, kc * 128:(kc + 1) * 128],
                    in_=xn_buf[:, (t % 2) * DM + kc * 128:(t % 2) * DM + (kc + 1) * 128],
                    identity=ident_b[:]), deps=d, signal=(kc == KC - 1), tag=f"tr{tag}")
            ev_evac = p.step("scalar", lambda e, t=t: e.copy(
                out=xnT.ap().rearrange("p (k c) -> p k c", k=KC, c=2048)[:, :, t * 128:(t + 1) * 128],
                in_=psB16.ap().rearrange("p (k c) -> p k c", k=KC, c=128)),
                deps=[ev_tr], signal=True, tag=f"tre{tag}")
            last_psB[0] = ev_evac
            return ev_tr, ev_evac

        ev_xn1T = [None] * NT
        ev_tr1 = [None] * NT

        def emit_ln1(t):
            war = ev_tr1[t - 2] if t >= 2 else None
            ev_ts = emit_ln(t, [ev_x[t]], war, f"1.{t}")
            ev_tr1[t], ev_xn1T[t] = emit_transposes(t, ev_ts, f"1.{t}")

        ev_vones = []
        for par in range(2):
            ev_vones.append(p.step("vector", lambda e, par=par: e.memset(
                vat.ap()[:, par * 1040:par * 1040 + 1032].rearrange(
                    "p (pr c) -> p pr c", pr=NP, c=129)[:, :, 128:129], 1.0),
                signal=True, tag=f"vones{par}"))

        ctx_bank_off = lambda pr: (2 + pr // 2, (pr % 2) * 130)
        ev_kv_evac = {}
        ev_ctx_tile = [None] * NT
        emit_ln1(0)
        emit_ln1(1)
        for t in range(NT):
            for g in range(4):
                bank = psA[g % 2]
                ev = None
                for kc in range(KC):
                    d = [ev_wqkv[kc]]
                    if kc == 0:
                        if g >= 2:
                            d.append(ev_kv_evac.get((t, g - 2)))
                        elif t > 0:
                            d.append(ev_kv_evac.get((t - 1, g + 2)))
                        d.append(ev_xn1T[t])
                    ev = p.step("tensor", lambda e, t=t, g=g, kc=kc, bank=bank: e.matmul(
                        bank[:, 0:512],
                        lhsT=xnT[:, kc * 2048 + t * 128: kc * 2048 + (t + 1) * 128],
                        rhs=rA[:, kc * 3072 + DM + g * 512: kc * 3072 + DM + (g + 1) * 512],
                        start=(kc == 0), stop=(kc == KC - 1)),
                        deps=d, signal=(kc == KC - 1), tag=f"kv{t}.{g}")
                war_kv = ev_ctx_tile[t - 2] if t >= 2 else None
                if g < 2:
                    ev_kv_evac[(t, g)] = p.step("scalar", lambda e, t=t, g=g, bank=bank: e.activation(
                        out=kat[:, (t % 2) * 1024 + g * 512:(t % 2) * 1024 + (g + 1) * 512],
                        in_=bank[:, 0:512], func=AF.Relu),
                        deps=[ev, war_kv], signal=True, tag=f"kev{t}.{g}")
                else:
                    gg = g - 2
                    ev_kv_evac[(t, g)] = p.step("scalar", lambda e, t=t, gg=gg, bank=bank: e.copy(
                        out=vat.ap()[:, (t % 2) * 1040 + gg * 516:(t % 2) * 1040 + gg * 516 + 516].rearrange(
                            "p (pr c) -> p pr c", pr=4, c=129)[:, :, 0:128],
                        in_=bank.ap().rearrange("p (pr c) -> p pr c", pr=4, c=128)),
                        deps=[ev, war_kv, ev_vones[t % 2]], signal=True, tag=f"vev{t}.{gg}")
            ev = None
            for pr in range(NP):
                bi, off = ctx_bank_off(pr)
                d = [ev_kv_evac[(t, 0 if pr < 4 else 1)], ev_kv_evac[(t, 2 if pr < 4 else 3)]]
                ev = p.step("tensor", lambda e, t=t, pr=pr, bi=bi, off=off: e.matmul(
                    psA[bi][:, off:off + 129],
                    lhsT=kat[:, (t % 2) * 1024 + pr * 128:(t % 2) * 1024 + (pr + 1) * 128],
                    rhs=vat[:, (t % 2) * 1040 + pr * 129:(t % 2) * 1040 + (pr + 1) * 129],
                    start=(t == 0 and pr % 2 == 0), stop=(t == NT - 1 and pr % 2 == 1)),
                    deps=d, signal=(pr == NP - 1), tag=f"ctx{t}.{pr}")
            ev_ctx_tile[t] = ev
            if t + 2 < NT:
                emit_ln1(t + 2)

        if dbg:
            p.step("sync", lambda e: e.dma_start(out=dbg_xnT, in_=xnT[:]),
                   deps=[ev_xn1T[NT - 1]], dma=True, tag="dbgxnT")
            p.step("sync", lambda e: e.dma_start(out=dbg_kv[:, 0:2048], in_=kat[:]),
                   deps=[ev_ctx_tile[1]], dma=True, tag="dbgkv1")
            p.step("sync", lambda e: e.dma_start(out=dbg_kv[:, 2048:4128], in_=vat[:, 0:2080]),
                   deps=[ev_ctx_tile[1]], dma=True, tag="dbgkv2")

        ev_ccs0 = p.step("vector", lambda e: e.memset(ccs[:], 0.0), signal=True, tag="ccs0")
        ev_ctx_evac = []
        for b in range(4):
            p.step("vector", lambda e, b=b: e.tensor_copy(
                out=ccs[:, b * 260:b * 260 + 129], in_=psA[2 + b][:, 0:129]),
                deps=[ev_ctx_tile[NT - 1], ev_ccs0], tag=f"ctxevA{b}")
            ev_ctx_evac.append(p.step("vector", lambda e, b=b: e.tensor_copy(
                out=ccs[:, b * 260 + 130:b * 260 + 259], in_=psA[2 + b][:, 130:259]),
                deps=[ev_ctx_tile[NT - 1]], signal=True, tag=f"ctxevB{b}"))
        ev_cc1_out = p.step("gpsimd", lambda e: e.dma_start(out=cc_ctx_in, in_=ccs[:]),
                            deps=ev_ctx_evac, dma=True, tag="cc1out")
        ev_cc1 = p.step("gpsimd", lambda e: e.collective_compute(
            "AllReduce", ALU.add, replica_groups=[[0, 1], [2, 3], [4, 5], [6, 7]],
            ins=[cc_ctx_in.opt()], outs=[cc_ctx_out.opt()]),
            deps=[ev_cc1_out], cc=True, tag="AR1")
        ev_ctx_rb = p.step("gpsimd", lambda e: e.dma_start(out=ctx_all[:], in_=cc_ctx_out),
                           deps=[ev_cc1], dma=True, tag="cc1rb")
        ev_bd0 = p.step("vector", lambda e: e.memset(ctxbd[:], 0.0), signal=True)
        ev_ks0 = p.step("vector", lambda e: e.memset(ksb[:], 0.0), signal=True)
        ev_ctxbd = []
        ev_ksb = None
        for pr in range(NP):
            bi, off = ctx_bank_off(pr)
            src = (bi - 2) * 260 + off
            p.step("vector", lambda e, pr=pr, src=src: e.tensor_copy(
                out=ctxbd[0:64, pr * 128:pr * 128 + 64], in_=ctx_all[0:64, src:src + 64]),
                deps=[ev_ctx_rb, ev_bd0])
            ev_ctxbd.append(p.step("vector", lambda e, pr=pr, src=src: e.tensor_copy(
                out=ctxbd[64:128, pr * 128 + 64:pr * 128 + 128], in_=ctx_all[64:128, src + 64:src + 128]),
                signal=True, tag=f"ctxbd{pr}"))
            p.step("vector", lambda e, pr=pr, src=src: e.tensor_copy(
                out=ksb[0:64, pr * 16 + 2 * pr:pr * 16 + 2 * pr + 1],
                in_=ctx_all[0:64, src + 128:src + 129]),
                deps=[ev_ks0])
            ev_ksb = p.step("vector", lambda e, pr=pr, src=src: e.tensor_copy(
                out=ksb[64:128, pr * 16 + 2 * pr + 1:pr * 16 + 2 * pr + 2],
                in_=ctx_all[64:128, src + 128:src + 129]),
                signal=True, tag=f"ksb{pr}")

        if dbg:
            p.step("sync", lambda e: e.dma_start(out=dbg_ccs, in_=ccs[:]),
                   deps=[ev_cc1_out], dma=True, tag="dbgccs")
            p.step("sync", lambda e: e.dma_start(out=dbg_ctxall, in_=ctx_all[:]),
                   deps=[ev_ctx_rb], dma=True, tag="dbgctxall")
            p.step("sync", lambda e: e.dma_start(out=dbg_ctxbd[:, 0:1024], in_=ctxbd[:]),
                   deps=[ev_ctxbd[NP - 1]], dma=True, tag="dbgctxbd")

        ev_q_evac = {}
        for nch in range(4):
            for oc in range(KC):
                bank = psA[oc % 2]
                ev = None
                for kc in range(KC):
                    d = [ev_wqkv[kc]]
                    if kc == 0:
                        prev = ev_q_evac.get((nch, oc - 2))
                        if prev is None and nch > 0:
                            prev = ev_q_evac.get((nch - 1, oc + 6))
                        if prev is None:
                            prev = ev_kv_evac[(NT - 1, 2 + (oc % 2))]
                        d.append(prev)
                    ev = p.step("tensor", lambda e, nch=nch, oc=oc, kc=kc, bank=bank: e.matmul(
                        bank[:, 0:512],
                        lhsT=rA[:, kc * 3072 + oc * 128: kc * 3072 + (oc + 1) * 128],
                        rhs=xnT[:, kc * 2048 + nch * 512: kc * 2048 + (nch + 1) * 512],
                        start=(kc == 0), stop=(kc == KC - 1)),
                        deps=d, signal=(kc == KC - 1), tag=f"q{nch}.{oc}")
                ev_q_evac[(nch, oc)] = p.step("scalar", lambda e, nch=nch, oc=oc, bank=bank: e.activation(
                    out=rB[:, oc * 2048 + nch * 512: oc * 2048 + (nch + 1) * 512],
                    in_=bank[:, 0:512], func=AF.Relu),
                    deps=[ev], signal=True, tag=f"qev{nch}.{oc}")
        ev_q_mm_done = ev

        ev_wproj = p.step("gpsimd", lambda e: e.dma_start(
            out=rA[:, 0:8192], in_=wproj_t),
            deps=[ev_q_mm_done], dma=True, tag="wproj")

        ev_zev = [None] * 4
        for ch in range(4):
            ev_dmm2 = None
            for pr in range(NP):
                d = [ev_ksb, ev_q_evac[(ch, pr)]]
                if pr == 0 and ch > 0:
                    d.append(ev_zev[ch - 1])
                ev_dmm2 = p.step("tensor", lambda e, ch=ch, pr=pr: e.matmul(
                    psA[6][0:16, 0:512],
                    lhsT=ksb[:, pr * 16:(pr + 1) * 16],
                    rhs=rB[:, pr * 2048 + ch * 512: pr * 2048 + (ch + 1) * 512],
                    start=(pr == 0), stop=(pr == NP - 1)),
                    deps=d, signal=(pr == NP - 1), tag=f"den{ch}.{pr}")
            ev_zev[ch] = p.step("scalar", lambda e, ch=ch: e.activation(
                out=kat[0:16, ch * 512:(ch + 1) * 512], in_=psA[6][0:16, 0:512],
                func=AF.Copy, bias=Z_EPS),
                deps=[ev_dmm2, ev_ctx_tile[NT - 1] if ch == 0 else None],
                signal=True, tag=f"zev{ch}")
        def _zrec(e):
            with nc.allow_low_precision(reason="z=1/(den+eps): den carries bf16 "
                                        "matmul-input rounding already; bar has 3x margin"):
                return e.reciprocal(out=kat[32:48, 0:2048], in_=kat[0:16, 0:2048])
        ev_zrec = p.step("vector", _zrec, deps=ev_zev, signal=True, tag="zrec")
        ev_ztr = None
        for t in range(NT):
            d = [ev_zrec, ev_idb]
            if t == 0:
                d.append(last_psB[0])
            ev_ztr = p.step("tensor", lambda e, t=t: e.transpose(
                psB16[:, t * 16:(t + 1) * 16],
                in_=kat[32:48, t * 128:(t + 1) * 128],
                identity=ident_b[32:48, 32:48]), deps=d, signal=(t == NT - 1), tag=f"ztr{t}")
        ev_ztok = p.step("vector", lambda e: e.tensor_copy(
            out=ccs[:, 0:256], in_=psB16[:, 0:256]),
            deps=[ev_ztr, ev_cc1_out], signal=True, tag="ztok")
        last_psB[0] = ev_ztok

        ev_att_tt = {}
        ev_att_pair = {}
        ev_a7ev = [None] * NT
        ev_projmm = [None] * NT
        ev_resid = {}

        def emit_att(t):
            b0 = 2 + (t % 2) * 2
            ev_sig = {}
            for pr in range(NP):
                b = b0 + pr // 4
                qq = pr // 4
                d = [ev_ctxbd[pr], ev_q_evac[(t // 4, pr)]]
                if t < 2:
                    d.append(ev_ctx_evac[b - 2])
                else:
                    d.append(ev_att_tt[(t - 2, qq)])
                ev_mm = p.step("tensor", lambda e, t=t, pr=pr, b=b: e.matmul(
                    psA[b][:, (pr % 4) * 128:(pr % 4) * 128 + 128],
                    lhsT=rB[:, pr * 2048 + t * 128: pr * 2048 + (t + 1) * 128],
                    rhs=ctxbd[:, pr * 128:(pr + 1) * 128],
                    start=True, stop=True),
                    deps=d, signal=(pr % 4 == 3), tag=f"att{t}.{pr}")
                if pr % 4 == 3:
                    ev_sig[qq] = ev_mm
            if t == NT - 1:
                for pp in range(4):
                    ev_att_pair[pp] = ev_sig[0]
                    ev_att_pair[4 + pp] = ev_sig[1]
            war_kat = ev_a7ev[t - 2] if t >= 2 else ev_ctx_tile[NT - 1]
            for qq in range(2):
                b = b0 + qq
                ev_att_tt[(t, qq)] = p.step("vector", lambda e, t=t, b=b, qq=qq: e.tensor_tensor(
                    out=kat.ap()[:, (t % 2) * 1024 + qq * 512:(t % 2) * 1024 + (qq + 1) * 512].rearrange(
                        "p (pr h c) -> p pr h c", pr=4, h=2, c=64),
                    in0=psA[b].ap()[:, 0:512].rearrange("p (pr h c) -> p pr h c",
                                                        pr=4, h=2, c=64),
                    in1=ccs.ap()[:, t * 16 + qq * 8: t * 16 + (qq + 1) * 8]
                        .rearrange("p (a c) -> p a c", a=4, c=2)
                        .unsqueeze(3).broadcast_to((128, 4, 2, 64)),
                    op=ALU.mult),
                    deps=[ev_sig[qq], war_kat, ev_ztok], signal=True, tag=f"attTT{t}.{qq}")

        def emit_proj(t):
            ev_tr = None
            for oc in range(KC):
                d = []
                if oc == 0:
                    d = [ev_att_tt[(t, 0)], ev_att_tt[(t, 1)], last_psB[0]]
                ev_tr = p.step("tensor", lambda e, t=t, oc=oc: e.transpose(
                    psB16[:, oc * 128:(oc + 1) * 128],
                    in_=kat[:, (t % 2) * 1024 + oc * 128:(t % 2) * 1024 + (oc + 1) * 128],
                    identity=ident_b[:]), deps=d, signal=(oc == KC - 1), tag=f"a7tr{t}")
            war_vat = ev_projmm[t - 2] if t >= 2 else ev_ctx_tile[NT - 1]
            ev_a7ev[t] = p.step("scalar", lambda e, t=t: e.copy(
                out=vat[:, (t % 2) * 1040:(t % 2) * 1040 + 1024], in_=psB16[:, 0:1024]),
                deps=[ev_tr, war_vat], signal=True, tag=f"a7ev{t}")
            last_psB[0] = ev_a7ev[t]
            ev_pm = {0: None, 1: None}
            for dc in range(KC):
                for hh in range(2):
                    d = [ev_a7ev[t], ev_wproj]
                    if dc == 0:
                        if t == 0:
                            d.append(ev_q_evac[(3, 6 + hh)])
                        else:
                            d.append(ev_resid[(t - 1, hh)])
                    p.step("tensor", lambda e, t=t, dc=dc, hh=hh: e.matmul(
                        psA[hh][:, 0:512],
                        lhsT=vat[:, (t % 2) * 1040 + dc * 128:(t % 2) * 1040 + (dc + 1) * 128],
                        rhs=rA[:, dc * 1024 + hh * 512: dc * 1024 + (hh + 1) * 512],
                        start=(dc == 0), stop=False),
                        deps=d, tag=f"proj{t}.{dc}.{hh}")
            for hh in range(2):
                ev_pm[hh] = p.step("tensor", lambda e, t=t, hh=hh: e.matmul(
                    psA[hh][:, 0:512],
                    lhsT=onesrow_b[:],
                    rhs=biasrow_b[:, hh * 512:(hh + 1) * 512],
                    start=False, stop=True),
                    deps=[ev_orb, ev_bias], signal=True, tag=f"bias{t}.{hh}")
            ev_projmm[t] = ev_pm[1]
            for hh in range(2):
                ev_resid[(t, hh)] = p.step("vector", lambda e, t=t, hh=hh: e.tensor_tensor(
                    out=x_sb[:, t * DM + hh * 512: t * DM + (hh + 1) * 512],
                    in0=psA[hh][:, 0:512],
                    in1=x_sb[:, t * DM + hh * 512: t * DM + (hh + 1) * 512],
                    op=ALU.add),
                    deps=[ev_pm[hh]], signal=True, tag=f"res{t}.{hh}")

        for t in range(NT + 1):
            if t < NT:
                emit_att(t)
            if t >= 1:
                emit_proj(t - 1)

        if dbg:
            p.step("sync", lambda e: e.dma_start(out=dbg_qT, in_=rB[:]),
                   deps=[ev_q_evac[(3, KC - 1)]], dma=True, tag="dbgqT")
            p.step("sync", lambda e: e.dma_start(out=dbg_att, in_=kat[:]),
                   deps=[ev_att_tt[(1, q)] for q in range(2)] + [ev_att_tt[(0, q)] for q in range(2)],
                   dma=True, tag="dbgatt")
            p.step("sync", lambda e: e.dma_start(out=dbg_z, in_=ccs[:, 0:256]),
                   deps=[ev_ztok], dma=True, tag="dbgz")

        if upto == "attn":
            ev_store = []
            for t in range(NT):
                ev_store.append(p.step("sync", lambda e, t=t: e.dma_start(
                    out=y_out[t * TPB:(t + 1) * TPB, :], in_=x_sb[:, t * DM:(t + 1) * DM]),
                    deps=[ev_resid[(t, 0)], ev_resid[(t, 1)]], dma=True, tag=f"st{t}"))
            p.wait_only("sync", ev_store, tag="final")
            p.emit(block, sems)
            return nc

        wd_col = lambda e_: 16384 if e_ % 2 == 0 else 8192
        wd_buf = lambda e_: rA if e_ % 2 == 0 else rB

        def emit_wd_dma(e_, deps):
            return p.step("gpsimd", lambda e, e_=e_: e.dma_start(
                out=wd_buf(e_)[:, wd_col(e_):wd_col(e_) + 8192],
                in_=wd_r[e_]),
                deps=deps, dma=True, tag=f"wd{e_}")

        def emit_wgu_dma(e_, g, deps):
            return p.step("gpsimd", lambda e, e_=e_, g=g: e.dma_start(
                out=rB[:, (g % 2) * 4096:(g % 2) * 4096 + 4096],
                in_=wgu_r[e_, g]),
                deps=deps, dma=True, tag=f"wgu{e_}.{g}")

        ev_wd = [None] * (NE + 1)
        ev_wgu = {}
        ev_wd[0] = emit_wd_dma(0, [ev_q_mm_done, ev_wproj])
        ev_wgu[(0, 0)] = emit_wgu_dma(0, 0, [ev_att_pair[0], ev_att_pair[1]])
        ev_wgu[(0, 1)] = emit_wgu_dma(0, 1, [ev_att_pair[2], ev_att_pair[3]])
        ev_wd[1] = emit_wd_dma(1, [ev_att_pair[4], ev_att_pair[5], ev_att_pair[6],
                                   ev_att_pair[7]])

        ev_xn2T = [None] * NT
        ev_cnt_add = [None] * NT
        ev_lg_evac = [None] * NT
        ev_cnt0 = p.step("vector", lambda e: e.memset(cnt_sb[:], 0.0), signal=True)
        ev_tr2 = [None] * NT
        for t in range(NT):
            war = ev_tr2[t - 2] if t >= 2 else None
            ev_ts = emit_ln(t, [ev_resid[(t, 0)], ev_resid[(t, 1)]], war, f"2.{t}")
            ev_tr2[t], ev_xn2T[t] = emit_transposes(t, ev_ts, f"2.{t}")
            ev_lmm = None
            for kc in range(KC):
                d = [ev_gate, ev_xn2T[t]]
                if kc == 0 and t > 0:
                    d.append(ev_cnt_add[t - 1])
                ev_lmm = p.step("tensor", lambda e, t=t, kc=kc: e.matmul(
                    psA[6][:, 0:8],
                    lhsT=xnT[:, kc * 2048 + t * 128: kc * 2048 + (t + 1) * 128],
                    rhs=gate_sb[:, kc * 8:(kc + 1) * 8],
                    start=(kc == 0), stop=(kc == KC - 1)),
                    deps=d, signal=(kc == KC - 1), tag=f"lg{t}.{kc}")
            ev_lg_evac[t] = p.step("vector", lambda e: e.tensor_copy(out=lg[:], in_=psA[6][:, 0:8]),
                                   deps=[ev_lmm], signal=True, tag=f"lgev{t}")
            p.step("vector", lambda e: e.max(out=m8[:], in_=lg[:]))
            p.step("vector", lambda e: e.max_index(out=idxu[:], in_max=m8[:], in_values=lg[:]))
            p.step("vector", lambda e: e.tensor_copy(out=idxf[:], in_=idxu[:]))
            p.step("vector", lambda e, t=t: e.tensor_copy(out=i12a[:, 2 * t:2 * t + 2], in_=idxf[:, 0:2]))
            p.step("vector", lambda e, t=t: e.tensor_tensor(
                out=da[:, t:t + 1], in0=m8[:, 0:1], in1=m8[:, 1:2], op=ALU.subtract))
            ev_oh = p.step("vector", lambda e: e.tensor_scalar(
                out=ohbf[:], in0=cst[:, C_IOTA0:C_IOTA0 + 8], scalar1=idxf[:, 0:1],
                scalar2=None, op0=ALU.is_equal), signal=True, tag=f"oh{t}")
            ev_cmm = p.step("tensor", lambda e: e.matmul(
                psA[6][0:1, 16:24], lhsT=onescol_b[:], rhs=ohbf[:],
                start=True, stop=True),
                deps=[ev_oh, ev_lg_evac[t], ev_ocb], signal=True, tag=f"cmm{t}")
            ev_cnt_add[t] = p.step("vector", lambda e: e.tensor_tensor(
                out=cnt_sb[:], in0=psA[6][0:1, 16:24], in1=cnt_sb[:], op=ALU.add),
                deps=[ev_cmm, ev_cnt0], signal=True, tag=f"cadd{t}")

        ev_cnt_out = p.step("gpsimd", lambda e: e.dma_start(out=cc_cnt_in, in_=cnt_sb[:]),
                            deps=[ev_cnt_add[NT - 1]], dma=True, tag="cc2out")
        ev_cc2 = p.step("gpsimd", lambda e: e.collective_compute(
            "AllReduce", ALU.add, replica_groups=[list(range(N_CORES))],
            ins=[cc_cnt_in.opt()], outs=[cc_cnt_out.opt()]),
            deps=[ev_cnt_out], cc=True, tag="AR2")
        ev_cnt_rb = p.step("gpsimd", lambda e: e.dma_start(out=cnt_row[:], in_=cc_cnt_out),
                           deps=[ev_cc2], dma=True, tag="cc2rb")

        routing_ev = {}

        def emit_routing(war_psA6):
            ev_csum = p.step("tensor", lambda e: e.matmul(
                psA[6][0:NE, 24:25], lhsT=cnt_row[:], rhs=onesrow_f[0:1, 0:1],
                start=True, stop=True),
                deps=[ev_cnt_rb, ev_cnt_add[NT - 1], ev_orf, war_psA6],
                signal=True, tag="csum")
            ev_c8 = p.step("vector", lambda e: e.tensor_copy(out=cnt8[:, 0:1], in_=psA[6][0:NE, 24:25]),
                           deps=[ev_csum], signal=True)
            p.step("vector", lambda e: e.tensor_scalar(
                out=cnt8[:, 1:2], in0=cnt8[:, 0:1], scalar1=0.0, scalar2=None, op0=ALU.is_gt))
            ev_prim = p.step("vector", lambda e: e.tensor_scalar(
                out=cnt8[:, 4:5], in0=cnt8[:, 1:2], scalar1=1.0, scalar2=-1.0,
                op0=ALU.subtract, op1=ALU.mult), signal=True, tag="nprim")
            ev_prmm = p.step("tensor", lambda e: e.matmul(
                psA[6][0:1, 32:40], lhsT=cnt8[0:NE, 1:2], rhs=ident8[:],
                start=True, stop=True), deps=[ev_prim, ev_i8], signal=True, tag="prmm")
            ev_prow = p.step("vector", lambda e: e.tensor_copy(out=prow[:], in_=psA[6][0:1, 32:40]),
                             deps=[ev_prmm], signal=True)
            ev_pmbm = p.step("tensor", lambda e: e.matmul(
                psA[6][0:NE, 40:48], lhsT=onesrow_f[0:1, 0:NE], rhs=prow[:],
                start=True, stop=True), deps=[ev_prow, ev_orf], signal=True, tag="pmbm")
            p.step("vector", lambda e: e.tensor_copy(out=pmb[:], in_=psA[6][0:NE, 40:48]),
                   deps=[ev_pmbm])
            p.step("vector", lambda e: e.scalar_tensor_tensor(
                out=msim[:], in0=pmb[:], scalar=1e9, in1=sim_sb[:], op0=ALU.mult, op1=ALU.add),
                deps=[ev_simld])
            p.step("vector", lambda e: e.tensor_scalar(
                out=msim[:], in0=msim[:], scalar1=1e9, scalar2=None, op0=ALU.subtract))
            p.step("vector", lambda e: e.max(out=b8[:], in_=msim[:]))
            p.step("vector", lambda e: e.max_index(out=bidxu[:], in_max=b8[:], in_values=msim[:]))
            p.step("vector", lambda e: e.tensor_copy(out=bidxf[:], in_=bidxu[:]))
            p.step("vector", lambda e: e.tensor_scalar(
                out=cnt8[:, 5:6], in0=b8[0:NE, 0:1], scalar1=cst[0:NE, C_THR:C_THR + 1],
                scalar2=None, op0=ALU.is_ge))
            p.step("vector", lambda e: e.tensor_tensor(
                out=cnt8[:, 2:3], in0=cnt8[:, 5:6], in1=cnt8[:, 4:5], op=ALU.mult))
            p.step("vector", lambda e: e.tensor_scalar(
                out=cnt8[:, 6:7], in0=cnt8[:, 2:3], scalar1=1.0, scalar2=-1.0,
                op0=ALU.subtract, op1=ALU.mult))
            p.step("vector", lambda e: e.tensor_tensor(
                out=cnt8[:, 7:8], in0=bidxf[0:NE, 0:1], in1=cnt8[:, 2:3], op=ALU.mult))
            p.step("vector", lambda e: e.tensor_tensor(
                out=cnt8[:, 6:7], in0=cst[0:NE, C_PIDX:C_PIDX + 1], in1=cnt8[:, 6:7], op=ALU.mult))
            ev_em = p.step("vector", lambda e: e.tensor_tensor(
                out=cnt8[:, 3:4], in0=cnt8[:, 7:8], in1=cnt8[:, 6:7], op=ALU.add),
                signal=True, tag="em")
            ev_emm = p.step("tensor", lambda e: e.matmul(
                psA[6][0:1, 48:56], lhsT=cnt8[0:NE, 3:4], rhs=ident8[:],
                start=True, stop=True), deps=[ev_em], signal=True, tag="emm")
            ev_emrow = p.step("vector", lambda e: e.tensor_copy(out=emrow[:], in_=psA[6][0:1, 48:56]),
                              deps=[ev_emm], signal=True)
            ev_embm = p.step("tensor", lambda e: e.matmul(
                psA[6][0:128, 56:64], lhsT=onesrow_f[:], rhs=emrow[:],
                start=True, stop=True), deps=[ev_emrow], signal=True, tag="embm")
            ev_emb = p.step("vector", lambda e: e.tensor_copy(out=emb[:], in_=psA[6][0:128, 56:64]),
                            deps=[ev_embm], signal=True)

            iota_b = cst.ap()[:, C_IOTA0:C_IOTA0 + 8].unsqueeze(1).broadcast_to((128, NT, NE))
            i1_b = i12a.ap().rearrange("p (t k) -> p t k", t=NT, k=2)[:, :, 0:1].broadcast_to((128, NT, NE))
            i2_b = i12a.ap().rearrange("p (t k) -> p t k", t=NT, k=2)[:, :, 1:2].broadcast_to((128, NT, NE))
            oh3 = oh_scr.ap().rearrange("p (t e) -> p t e", t=NT, e=NE)
            oh23 = oh2_scr.ap().rearrange("p (t e) -> p t e", t=NT, e=NE)
            dense3 = dense.ap().rearrange("p (t e) -> p t e", t=NT, e=NE)
            emb_b = emb.ap().unsqueeze(1).broadcast_to((128, NT, NE))
            p.step("vector", lambda e: e.tensor_tensor(out=oh3, in0=iota_b, in1=i1_b, op=ALU.is_equal),
                   deps=[ev_emb])
            p.step("vector", lambda e: e.tensor_tensor(out=oh3, in0=oh3, in1=emb_b, op=ALU.mult))
            p.step("vector", lambda e: e.tensor_reduce(out=r1a[:], in_=oh3, axis=AX.X, op=ALU.add))
            p.step("vector", lambda e: e.tensor_tensor(out=oh23, in0=iota_b, in1=i2_b, op=ALU.is_equal))
            p.step("vector", lambda e: e.tensor_tensor(out=oh23, in0=oh23, in1=emb_b, op=ALU.mult))
            p.step("vector", lambda e: e.tensor_reduce(out=r2a[:], in_=oh23, axis=AX.X, op=ALU.add))
            ev_da_done = p.step("vector", lambda e: e.tensor_copy(out=w2a[:], in_=da[:]), signal=True)
            ev_sig = p.step("scalar", lambda e: e.activation(out=w1a[:], in_=w2a[:], func=AF.Sigmoid),
                            deps=[ev_da_done], signal=True, tag="sigmoid")
            p.step("vector", lambda e: e.tensor_scalar(
                out=w2a[:], in0=w1a[:], scalar1=1.0, scalar2=-1.0, op0=ALU.subtract, op1=ALU.mult),
                deps=[ev_sig])
            r1_b = r1a.ap().unsqueeze(2).broadcast_to((128, NT, NE))
            r2_b = r2a.ap().unsqueeze(2).broadcast_to((128, NT, NE))
            w1_b = w1a.ap().unsqueeze(2).broadcast_to((128, NT, NE))
            w2_b = w2a.ap().unsqueeze(2).broadcast_to((128, NT, NE))
            p.step("vector", lambda e: e.tensor_tensor(out=oh3, in0=iota_b, in1=r1_b, op=ALU.is_equal))
            p.step("vector", lambda e: e.tensor_tensor(out=dense3, in0=oh3, in1=w1_b, op=ALU.mult))
            p.step("vector", lambda e: e.tensor_tensor(out=oh23, in0=iota_b, in1=r2_b, op=ALU.is_equal))
            p.step("vector", lambda e: e.tensor_tensor(out=oh23, in0=oh23, in1=w2_b, op=ALU.mult))
            ev_dense = p.step("vector", lambda e: e.tensor_tensor(
                out=dense3, in0=dense3, in1=oh23, op=ALU.add), signal=True, tag="dense")
            routing_ev["emb"] = ev_emb
            routing_ev["dense"] = ev_dense


        def slot_ap(s, width=512):
            if s < 7:
                return psA[s][:, 0:width]
            return psBf32[:, 0:width]

        ev_gu_fc_last = {}
        ev_sE = {}
        ev_uE = {}
        ev_down_last = [None] * NE
        ev_stt = {}

        last_slot_epilogue = {
            0: ev_resid[(NT - 1, 0)],
            1: ev_resid[(NT - 1, 1)],
            2: ev_att_tt[(NT - 2, 0)],
            3: ev_att_tt[(NT - 2, 1)],
            4: ev_att_tt[(NT - 1, 0)],
            5: ev_att_tt[(NT - 1, 1)],
            6: ev_cnt_add[NT - 1],
            7: last_psB[0],
        }

        def emit_gu(e_):
            for fc in range(FCG):
                g = fc // 4
                ev_fc = None
                for kc in range(KC):
                    for nch in range(4):
                        s = (fc % 2) * 4 + nch
                        d = [ev_wgu[(e_, g)]]
                        if kc == 0:
                            d.append(last_slot_epilogue.get(s))
                            d.append(ev_xn2T[4 * nch + 3])
                        ev_fc = p.step("tensor", lambda e, fc=fc, kc=kc, nch=nch, s=s: e.matmul(
                            slot_ap(s),
                            lhsT=rB[:, (fc // 4 % 2) * 4096 + (fc % 4) * 1024 + kc * 128:
                                    (fc // 4 % 2) * 4096 + (fc % 4) * 1024 + (kc + 1) * 128],
                            rhs=xnT[:, kc * 2048 + nch * 512: kc * 2048 + (nch + 1) * 512],
                            start=(kc == 0), stop=(kc == KC - 1)),
                            deps=d, signal=(kc == KC - 1), tag=f"gu{e_}.{fc}.{nch}")
                        if kc == KC - 1:
                            if fc < 8:
                                dd = [ev_fc]
                                if e_ > 0:
                                    dd.append(ev_down_last[e_ - 1])
                                ev_sE[(e_, fc, nch)] = p.step("scalar",
                                    lambda e, fc=fc, nch=nch, s=s: e.activation(
                                        out=rA[:, fc * 2048 + nch * 512: fc * 2048 + (nch + 1) * 512],
                                        in_=slot_ap(s), func=AF.Silu),
                                    deps=dd, signal=True, tag=f"silu{e_}.{fc}.{nch}")
                                last_slot_epilogue[s] = ev_sE[(e_, fc, nch)]
                            else:
                                fcc = fc - 8
                                ev_uE[(e_, fcc, nch)] = p.step("vector",
                                    lambda e, fcc=fcc, nch=nch, s=s: e.tensor_tensor(
                                        out=rA[:, fcc * 2048 + nch * 512: fcc * 2048 + (nch + 1) * 512],
                                        in0=rA[:, fcc * 2048 + nch * 512: fcc * 2048 + (nch + 1) * 512],
                                        in1=slot_ap(s), op=ALU.mult),
                                    deps=[ev_fc, ev_sE[(e_, fcc, nch)]], signal=True,
                                    tag=f"umul{e_}.{fcc}.{nch}")
                                last_slot_epilogue[s] = ev_uE[(e_, fcc, nch)]
                ev_gu_fc_last[(e_, fc)] = ev_fc
                if fc == 3:
                    ev_wgu[(e_, 2)] = emit_wgu_dma(e_, 2, [ev_fc])
                elif fc == 7:
                    ev_wgu[(e_, 3)] = emit_wgu_dma(e_, 3, [ev_fc])
                elif fc == 11 and e_ + 1 < NE:
                    ev_wgu[(e_ + 1, 0)] = emit_wgu_dma(e_ + 1, 0, [ev_fc])
                elif fc == 15 and e_ + 1 < NE:
                    ev_wgu[(e_ + 1, 1)] = emit_wgu_dma(e_ + 1, 1, [ev_fc])

        def emit_down(e_):
            ev_dmm = None
            for t in range(NT):
                pA0, pA1 = (0, 1) if t % 2 == 0 else (2, 3)
                for fc in range(KC):
                    for hh in range(2):
                        d = [ev_wd[e_], ev_uE[(e_, fc, t // 4)]]
                        if fc == 0 and hh == 0:
                            if t < 2:
                                d += [last_slot_epilogue[pA0], last_slot_epilogue[pA1]]
                            else:
                                d += [ev_stt[(e_, t - 2, 0)], ev_stt[(e_, t - 2, 1)]]
                        ev_dmm = p.step("tensor",
                            lambda e, e_=e_, t=t, fc=fc, hh=hh, pA=(pA0 if hh == 0 else pA1): e.matmul(
                                psA[pA][:, 0:512],
                                lhsT=rA[:, fc * 2048 + t * 128: fc * 2048 + (t + 1) * 128],
                                rhs=wd_buf(e_)[:, wd_col(e_) + fc * 1024 + hh * 512:
                                               wd_col(e_) + fc * 1024 + (hh + 1) * 512],
                                start=(fc == 0), stop=(fc == KC - 1)),
                            deps=d, signal=(fc == KC - 1), tag=f"dn{e_}.{t}.{hh}")
                for hh in range(2):
                    pA = pA0 if hh == 0 else pA1
                    ev_stt[(e_, t, hh)] = p.step("vector",
                        lambda e, e_=e_, t=t, hh=hh, pA=pA: e.scalar_tensor_tensor(
                            out=x_sb[:, t * DM + hh * 512: t * DM + (hh + 1) * 512],
                            in0=psA[pA][:, 0:512],
                            scalar=dense[:, t * NE + e_: t * NE + e_ + 1],
                            in1=x_sb[:, t * DM + hh * 512: t * DM + (hh + 1) * 512],
                            op0=ALU.mult, op1=ALU.add),
                        deps=[ev_dmm, routing_ev["dense"]], signal=True, tag=f"stt{e_}.{t}.{hh}")
            ev_down_last[e_] = ev_dmm
            last_slot_epilogue[0] = ev_stt[(e_, NT - 2, 0)]
            last_slot_epilogue[1] = ev_stt[(e_, NT - 2, 1)]
            last_slot_epilogue[2] = ev_stt[(e_, NT - 1, 0)]
            last_slot_epilogue[3] = ev_stt[(e_, NT - 1, 1)]
            if e_ + 2 <= NE - 1:
                ev_wd[e_ + 2] = emit_wd_dma(e_ + 2, [ev_down_last[e_]])

        emit_gu(0)
        emit_routing(last_slot_epilogue[6])
        last_slot_epilogue[6] = routing_ev["emb"]
        emit_down(0)
        for e_ in range(1, NE):
            emit_gu(e_)
            emit_down(e_)

        ev_store = []
        for t in range(NT):
            ev_store.append(p.step("sync", lambda e, t=t: e.dma_start(
                out=y_out[t * TPB:(t + 1) * TPB, :], in_=x_sb[:, t * DM:(t + 1) * DM]),
                deps=[ev_stt[(NE - 1, t, 0)], ev_stt[(NE - 1, t, 1)]], dma=True, tag=f"st{t}"))
        p.wait_only("sync", ev_store, tag="final")

        p.emit(block, sems)

    return nc



def _prep_inputs(L0_attn_x, L0_attn_w_qkv, L0_attn_w_proj, L0_attn_b_proj,
                 L0_ffn_gate_weight, L0_ffn_gate_up_proj, L0_ffn_down_proj,
                 L0_ffn_similarity_matrix):
    x = np.ascontiguousarray(
        np.asarray(L0_attn_x, dtype=np.float32).reshape(N_CORES, NT, TPB, DM)
        .transpose(0, 2, 1, 3).reshape(N_CORES, TPB, NT * DM))
    wqkv = np.ascontiguousarray(np.asarray(L0_attn_w_qkv, dtype=np.float32).T).reshape(KC, 128, 3 * DM)
    wproj = np.ascontiguousarray(
        np.asarray(L0_attn_w_proj, dtype=np.float32).T.reshape(KC, 128, DM)
        .transpose(1, 0, 2).reshape(128, KC * DM))
    brow = np.ascontiguousarray(np.asarray(L0_attn_b_proj, dtype=np.float32).reshape(1, DM))
    gw = np.asarray(L0_ffn_gate_weight, dtype=np.float32)
    gate_in = np.ascontiguousarray(gw.T.reshape(KC, 128, NE).transpose(1, 0, 2).reshape(128, KC * NE))
    gu = np.asarray(L0_ffn_gate_up_proj, dtype=np.float32)
    wgu = np.ascontiguousarray(
        gu.reshape(NE, 4, 4, 128, KC, 128).transpose(0, 1, 5, 2, 4, 3)
        .reshape(NE, 4, 128, 4096))
    dn = np.asarray(L0_ffn_down_proj, dtype=np.float32)
    wd = np.ascontiguousarray(
        dn.transpose(0, 2, 1).reshape(NE, KC, 128, DM).transpose(0, 2, 1, 3)
        .reshape(NE, 128, KC * DM))
    sim = np.ascontiguousarray(np.asarray(L0_ffn_similarity_matrix, dtype=np.float32))
    ident = np.eye(128, dtype=np.float32)
    onesrow = np.ones((1, 128), dtype=np.float32)

    in_maps = []
    for c in range(N_CORES):
        cstv = np.zeros((128, C_NCOLS), dtype=np.float32)
        cstv[:, C_LNEPS] = LN_EPS
        cstv[:, C_ZEPS] = Z_EPS
        cstv[:, C_THR] = THR
        cstv[:, C_ONE] = 1.0
        cstv[:, C_IOTA0:C_IOTA0 + 8] = np.arange(8, dtype=np.float32)[None, :]
        cstv[:, C_PIDX] = np.arange(128, dtype=np.float32)
        b = c // 2
        cstv[:, C_CSEL0 + 2 * b] = 1.0
        cstv[:, C_CSEL0 + 2 * b + 1] = 1.0
        in_maps.append({
            "x_in": x[c],
            "wqkv_t": wqkv,
            "wproj_t": wproj,
            "biasrow": brow,
            "gate_in": gate_in,
            "wgu_r": wgu,
            "wd_r": wd,
            "sim_in": sim,
            "cst_in": cstv,
            "ident_in": ident,
            "onesrow_in": onesrow,
        })
    return in_maps


def kernel(L0_attn_x, L0_attn_w_qkv, L0_attn_w_proj, L0_attn_b_proj,
           L0_ffn_gate_weight, L0_ffn_gate_up_proj, L0_ffn_down_proj,
           L0_ffn_similarity_matrix, _upto="full",
           _strict=frozenset(("vector", "scalar"))):
    in_maps = _prep_inputs(L0_attn_x, L0_attn_w_qkv, L0_attn_w_proj, L0_attn_b_proj,
                           L0_ffn_gate_weight, L0_ffn_gate_up_proj, L0_ffn_down_proj,
                           L0_ffn_similarity_matrix)
    nc = build(upto=_upto, strict=_strict)
    res = run_bass_kernel_spmd(nc, in_maps, core_ids=list(range(N_CORES)))
    y = np.concatenate([np.asarray(r["y"], dtype=np.float32) for r in res.results], axis=0)
    return y.reshape(4, 4096, DM)


# version 10
# speedup vs baseline: 1.0845x; 1.0435x over previous
import numpy as np
import ml_dtypes
from contextlib import ExitStack

import concourse.bass as bass
import concourse.mybir as mybir
from concourse.bass_utils import run_bass_kernel_spmd

F32 = mybir.dt.float32
BF16 = mybir.dt.bfloat16
U32 = mybir.dt.uint32
AF = mybir.ActivationFunctionType
ALU = mybir.AluOpType
AX = mybir.AxisListType

N_CORES = 8
TPB = 128
NT = 16
TOK = TPB * NT
DM = 1024
KC = 8
NP = 8
NE = 8
FCG = 16
LN_EPS = 1e-5
Z_EPS = float(np.finfo(np.float32).eps)
THR = 0.7

C_LNEPS, C_ZEPS, C_THR, C_ONE, C_IOTA0, C_PIDX, C_CSEL0 = 0, 1, 2, 3, 4, 12, 13
C_NCOLS = 24


class Prog:
    ENGINES = ("tensor", "vector", "scalar", "gpsimd", "sync")

    def __init__(self, strict=frozenset()):
        self.steps = {e: [] for e in self.ENGINES}
        self.counts = {e: 0 for e in self.ENGINES}
        self.counts["cc"] = 0
        self.strict = frozenset(strict) if not isinstance(strict, bool) else (
            frozenset(self.ENGINES) if strict else frozenset())
        self.last_ev = {e: None for e in self.ENGINES}

    def step(self, eng, fn, deps=(), signal=False, dma=False, cc=False, tag=""):
        strict = eng in self.strict
        if strict and not (dma or cc):
            signal = True
        amt = 16 if dma else (1 if (signal or cc) else 0)
        sig_key = "cc" if cc else eng
        dl = []
        deps = list(deps)
        if strict and self.last_ev[eng] is not None:
            deps.append(self.last_ev[eng])
        for d in deps:
            if d is None:
                continue
            de, dt, da = d
            if de == eng and not (da or strict):
                continue
            dl.append((de, dt))
        self.steps[eng].append((fn, dl, amt if not cc else ("cc", amt), tag))
        if amt:
            self.counts[sig_key] += amt
            ev = (sig_key, self.counts[sig_key], dma or cc)
            if not cc:
                self.last_ev[eng] = ev
            return ev
        return None

    def wait_only(self, eng, deps, tag="wait"):
        dl = []
        for d in deps:
            if d is None:
                continue
            de, dt, da = d
            if de != eng or da:
                dl.append((de, dt))
        self.steps[eng].append((None, dl, 0, tag))

    def emit(self, block, sems):
        for eng_name in self.ENGINES:
            steps = self.steps[eng_name]
            if not steps:
                continue

            def body(eng, steps=steps, eng_name=eng_name):
                waited = {}
                for fn, deps, amt, tag in steps:
                    for (dep_eng, target) in deps:
                        if waited.get(dep_eng, 0) < target:
                            eng.wait_ge(sems[dep_eng], target)
                            waited[dep_eng] = target
                    if fn is None:
                        continue
                    inst = fn(eng)
                    if amt:
                        if isinstance(amt, tuple):
                            inst.then_inc(sems["cc"], amt[1])
                        else:
                            inst.then_inc(sems[eng_name], amt)

            getattr(block, eng_name)(body)


def build(upto="full", dbg=False, strict=False):
    nc = bass.Bass("TRN2", target_bir_lowering=False, debug=False)

    x_in = nc.dram_tensor("x_in", [128, NT * DM], F32, kind="ExternalInput").ap()
    wqkv_t = nc.dram_tensor("wqkv_t", [KC, 128, 3 * DM], F32, kind="ExternalInput").ap()
    wproj_t = nc.dram_tensor("wproj_t", [128, KC * DM], F32, kind="ExternalInput").ap()
    biasrow = nc.dram_tensor("biasrow", [1, DM], F32, kind="ExternalInput").ap()
    gate_in = nc.dram_tensor("gate_in", [128, KC * NE], F32, kind="ExternalInput").ap()
    wgu_r = nc.dram_tensor("wgu_r", [NE, 4, 128, 4096], F32, kind="ExternalInput").ap()
    wd_r = nc.dram_tensor("wd_r", [NE, 128, KC * DM], F32, kind="ExternalInput").ap()
    sim_in = nc.dram_tensor("sim_in", [NE, NE], F32, kind="ExternalInput").ap()
    cst_in = nc.dram_tensor("cst_in", [128, C_NCOLS], F32, kind="ExternalInput").ap()
    ident_in = nc.dram_tensor("ident_in", [128, 128], F32, kind="ExternalInput").ap()
    onesrow_in = nc.dram_tensor("onesrow_in", [1, 128], F32, kind="ExternalInput").ap()

    y_out = nc.dram_tensor("y", [TOK, DM], F32, kind="ExternalOutput").ap()
    if dbg:
        dbg_xnT = nc.dram_tensor("dbg_xnT", [128, 16384], BF16, kind="ExternalOutput").ap()
        dbg_ccs = nc.dram_tensor("dbg_ccs", [128, 1040], F32, kind="ExternalOutput").ap()
        dbg_ctxall = nc.dram_tensor("dbg_ctxall", [128, 1040], F32, kind="ExternalOutput").ap()
        dbg_ctxbd = nc.dram_tensor("dbg_ctxbd", [128, 1040], BF16, kind="ExternalOutput").ap()
        dbg_att = nc.dram_tensor("dbg_att", [128, 2048], BF16, kind="ExternalOutput").ap()
        dbg_z = nc.dram_tensor("dbg_z", [128, 256], F32, kind="ExternalOutput").ap()
        dbg_qT = nc.dram_tensor("dbg_qT", [128, 16384], BF16, kind="ExternalOutput").ap()
        dbg_kv = nc.dram_tensor("dbg_kv", [128, 4128], BF16, kind="ExternalOutput").ap()

    cc_ctx_in = nc.dram_tensor("cc_ctx_in", [128, 1040], F32).ap()
    cc_ctx_out = nc.dram_tensor("cc_ctx_out", [128, 1040], F32).ap()
    cc_cnt_in = nc.dram_tensor("cc_cnt_in", [1, NE], F32).ap()
    cc_cnt_out = nc.dram_tensor("cc_cnt_out", [1, NE], F32).ap()

    p = Prog(strict=strict)

    with ExitStack() as ectx:
        sb = lambda name, shape, dt: ectx.enter_context(nc.sbuf_tensor(name, shape, dt))
        psalloc = lambda name, shape, dt: ectx.enter_context(nc.psum_tensor(name, shape, dt))

        x_sb = sb("x_sb", [128, NT * DM], F32)
        rA = sb("rA", [128, 24576], BF16)
        rB = sb("rB", [128, 16384], BF16)
        xnT = sb("xnT", [128, 16384], BF16)
        kat = sb("kat", [128, 2048], BF16)
        vat = sb("vat", [128, 2080], BF16)
        xn_buf = sb("xn_buf", [128, 2048], BF16)
        ccs = sb("ccs", [128, 1040], F32)
        ctx_all = sb("ctx_all", [128, 1040], F32)
        ctxbd = sb("ctxbd", [128, NP * 128], BF16)
        ksb = sb("ksb", [128, NP * 16], BF16)
        cst = sb("cst", [128, C_NCOLS], F32)
        ident_b = sb("ident_b", [128, 128], BF16)
        onesrow_f = sb("onesrow_f", [1, 128], F32)
        onesrow_b = sb("onesrow_b", [1, 128], BF16)
        onescol_b = sb("onescol_b", [128, 1], BF16)
        biasrow_b = sb("biasrow_b", [1, DM], BF16)
        gate_sb = sb("gate_sb", [128, KC * NE], BF16)
        sim_sb = sb("sim_sb", [NE, NE], F32)
        ident8 = sb("ident8", [NE, NE], F32)
        mv = sb("mv", [128, 16], F32)
        lg = sb("lg", [128, NE], F32)
        m8 = sb("m8", [128, NE], F32)
        idxu = sb("idxu", [128, NE], U32)
        idxf = sb("idxf", [128, NE], F32)
        i12a = sb("i12a", [128, 2 * NT], F32)
        da = sb("da", [128, NT], F32)
        w1a = sb("w1a", [128, NT], F32)
        w2a = sb("w2a", [128, NT], F32)
        oh_scr = sb("oh_scr", [128, NT * NE], F32)
        oh2_scr = sb("oh2_scr", [128, NT * NE], F32)
        r1a = sb("r1a", [128, NT], F32)
        r2a = sb("r2a", [128, NT], F32)
        dense = sb("dense", [128, NT * NE], F32)
        ohbf = sb("ohbf", [128, NE], BF16)
        cnt_sb = sb("cnt_sb", [1, NE], F32)
        cnt_row = sb("cnt_row", [1, NE], F32)
        cnt8 = sb("cnt8", [NE, 8], F32)
        prow = sb("prow", [1, NE], F32)
        pmb = sb("pmb", [NE, NE], F32)
        msim = sb("msim", [NE, NE], F32)
        b8 = sb("b8", [NE, NE], F32)
        bidxu = sb("bidxu", [NE, NE], U32)
        bidxf = sb("bidxf", [NE, NE], F32)
        emrow = sb("emrow", [1, NE], F32)
        emb = sb("emb", [128, NE], F32)
        z2 = sb("z2", [128, 16], F32)
        zr = sb("zr", [128, 16], F32)

        psA = [psalloc(f"psA{i}", [128, 512], F32) for i in range(7)]
        psB16 = psalloc("psB16", [128, 1024], BF16)
        psBf32 = psB16.ap().bitcast(F32)

        sems = {e: ectx.enter_context(nc.semaphore(f"s_{e}")) for e in list(Prog.ENGINES) + ["cc"]}
        block = ectx.enter_context(nc.Block())

        ev_cst = p.step("sync", lambda e: e.dma_start(out=cst[:], in_=cst_in), dma=True, tag="cst")
        ev_simld = p.step("sync", lambda e: e.dma_start(out=sim_sb[:], in_=sim_in), dma=True)
        ev_i8 = p.step("sync", lambda e: e.dma_start(out=ident8[:], in_=ident_in[0:NE, 0:NE]), dma=True)
        ev_orf = p.step("sync", lambda e: e.dma_start(out=onesrow_f[:], in_=onesrow_in), dma=True)
        ev_x = []
        for g in range(4):
            ev = p.step(
                "sync", lambda e, g=g: e.dma_start(out=x_sb[:, g * 4 * DM:(g + 1) * 4 * DM],
                                                   in_=x_in[:, g * 4 * DM:(g + 1) * 4 * DM]),
                dma=True, tag=f"x{g}")
            ev_x += [ev] * 4

        ev_idb = p.step("gpsimd", lambda e: e.dma_start(out=ident_b[:], in_=ident_in), dma=True)
        ev_orb = p.step("gpsimd", lambda e: e.dma_start(out=onesrow_b[:], in_=onesrow_in), dma=True)
        ev_ocb = p.step("gpsimd", lambda e: e.dma_start(out=onescol_b[:], in_=cst_in[:, C_ONE:C_ONE + 1],
                                                        allow_slow_non_contiguous=True), dma=True)
        ev_bias = p.step("gpsimd", lambda e: e.dma_start(out=biasrow_b[:], in_=biasrow), dma=True)
        ev_gate = p.step("gpsimd", lambda e: e.dma_start(out=gate_sb[:], in_=gate_in), dma=True)
        ev_wqkv = []
        for kc in range(KC):
            ev_wqkv.append(p.step(
                "gpsimd", lambda e, kc=kc: e.dma_start(out=rA[:, kc * 3072:(kc + 1) * 3072],
                                                       in_=wqkv_t[kc]),
                dma=True, tag=f"wqkv{kc}"))

        last_psB = [None]

        def emit_ln(t, deps_in, war_dep, tag):
            p.step("vector", lambda e, t=t: e.bn_stats(out=mv[:, 0:6], in_=x_sb[:, t * DM:t * DM + 512]),
                   deps=list(deps_in) + [war_dep, ev_cst], tag=f"bn{tag}")
            p.step("vector", lambda e, t=t: e.bn_stats(out=mv[:, 6:12], in_=x_sb[:, t * DM + 512:(t + 1) * DM]))
            ev_a = p.step("vector", lambda e: e.bn_aggr(out=mv[:, 12:14], in_=mv[:, 0:12]), signal=True)
            ev_q = p.step("scalar", lambda e: e.activation(out=mv[:, 14:15], in_=mv[:, 13:14],
                                                           func=AF.Sqrt, bias=cst[:, C_LNEPS:C_LNEPS + 1]),
                          deps=[ev_a], signal=True, tag=f"sqrt{tag}")
            p.step("vector", lambda e: e.reciprocal(out=mv[:, 15:16], in_=mv[:, 14:15]), deps=[ev_q])
            return p.step("vector", lambda e, t=t: e.tensor_scalar(
                out=xn_buf[:, (t % 2) * DM:(t % 2 + 1) * DM], in0=x_sb[:, t * DM:(t + 1) * DM],
                scalar1=mv[:, 12:13], scalar2=mv[:, 15:16],
                op0=ALU.subtract, op1=ALU.mult), signal=True, tag=f"ts{tag}")

        def emit_transposes(t, ev_ts, tag):
            ev_tr = None
            for kc in range(KC):
                d = [ev_ts, ev_idb, last_psB[0]] if kc == 0 else []
                ev_tr = p.step("tensor", lambda e, t=t, kc=kc: e.transpose(
                    psB16[:, kc * 128:(kc + 1) * 128],
                    in_=xn_buf[:, (t % 2) * DM + kc * 128:(t % 2) * DM + (kc + 1) * 128],
                    identity=ident_b[:]), deps=d, signal=(kc == KC - 1), tag=f"tr{tag}")
            ev_evac = p.step("scalar", lambda e, t=t: e.copy(
                out=xnT.ap().rearrange("p (k c) -> p k c", k=KC, c=2048)[:, :, t * 128:(t + 1) * 128],
                in_=psB16.ap().rearrange("p (k c) -> p k c", k=KC, c=128)),
                deps=[ev_tr], signal=True, tag=f"tre{tag}")
            last_psB[0] = ev_evac
            return ev_tr, ev_evac

        ev_xn1T = [None] * NT
        ev_tr1 = [None] * NT

        def emit_ln1(t):
            war = ev_tr1[t - 2] if t >= 2 else None
            ev_ts = emit_ln(t, [ev_x[t]], war, f"1.{t}")
            ev_tr1[t], ev_xn1T[t] = emit_transposes(t, ev_ts, f"1.{t}")

        ev_vones = []
        for par in range(2):
            ev_vones.append(p.step("vector", lambda e, par=par: e.memset(
                vat.ap()[:, par * 1040:par * 1040 + 1032].rearrange(
                    "p (pr c) -> p pr c", pr=NP, c=129)[:, :, 128:129], 1.0),
                signal=True, tag=f"vones{par}"))

        ctx_bank_off = lambda pr: (2 + pr // 2, (pr % 2) * 130)
        ev_kv_evac = {}
        ev_ctx_tile = [None] * NT
        emit_ln1(0)
        emit_ln1(1)
        for t in range(NT):
            for g in range(4):
                bank = psA[g % 2]
                ev = None
                for kc in range(KC):
                    d = [ev_wqkv[kc]]
                    if kc == 0:
                        if g >= 2:
                            d.append(ev_kv_evac.get((t, g - 2)))
                        elif t > 0:
                            d.append(ev_kv_evac.get((t - 1, g + 2)))
                        d.append(ev_xn1T[t])
                    ev = p.step("tensor", lambda e, t=t, g=g, kc=kc, bank=bank: e.matmul(
                        bank[:, 0:512],
                        lhsT=xnT[:, kc * 2048 + t * 128: kc * 2048 + (t + 1) * 128],
                        rhs=rA[:, kc * 3072 + DM + g * 512: kc * 3072 + DM + (g + 1) * 512],
                        start=(kc == 0), stop=(kc == KC - 1)),
                        deps=d, signal=(kc == KC - 1), tag=f"kv{t}.{g}")
                war_kv = ev_ctx_tile[t - 2] if t >= 2 else None
                if g < 2:
                    ev_kv_evac[(t, g)] = p.step("scalar", lambda e, t=t, g=g, bank=bank: e.activation(
                        out=kat[:, (t % 2) * 1024 + g * 512:(t % 2) * 1024 + (g + 1) * 512],
                        in_=bank[:, 0:512], func=AF.Relu),
                        deps=[ev, war_kv], signal=True, tag=f"kev{t}.{g}")
                else:
                    gg = g - 2
                    ev_kv_evac[(t, g)] = p.step("scalar", lambda e, t=t, gg=gg, bank=bank: e.copy(
                        out=vat.ap()[:, (t % 2) * 1040 + gg * 516:(t % 2) * 1040 + gg * 516 + 516].rearrange(
                            "p (pr c) -> p pr c", pr=4, c=129)[:, :, 0:128],
                        in_=bank.ap().rearrange("p (pr c) -> p pr c", pr=4, c=128)),
                        deps=[ev, war_kv, ev_vones[t % 2]], signal=True, tag=f"vev{t}.{gg}")
            ev = None
            for pr in range(NP):
                bi, off = ctx_bank_off(pr)
                d = [ev_kv_evac[(t, 0 if pr < 4 else 1)], ev_kv_evac[(t, 2 if pr < 4 else 3)]]
                ev = p.step("tensor", lambda e, t=t, pr=pr, bi=bi, off=off: e.matmul(
                    psA[bi][:, off:off + 129],
                    lhsT=kat[:, (t % 2) * 1024 + pr * 128:(t % 2) * 1024 + (pr + 1) * 128],
                    rhs=vat[:, (t % 2) * 1040 + pr * 129:(t % 2) * 1040 + (pr + 1) * 129],
                    start=(t == 0 and pr % 2 == 0), stop=(t == NT - 1 and pr % 2 == 1)),
                    deps=d, signal=(pr == NP - 1), tag=f"ctx{t}.{pr}")
            ev_ctx_tile[t] = ev
            if t + 2 < NT:
                emit_ln1(t + 2)

        if dbg:
            p.step("sync", lambda e: e.dma_start(out=dbg_xnT, in_=xnT[:]),
                   deps=[ev_xn1T[NT - 1]], dma=True, tag="dbgxnT")
            p.step("sync", lambda e: e.dma_start(out=dbg_kv[:, 0:2048], in_=kat[:]),
                   deps=[ev_ctx_tile[1]], dma=True, tag="dbgkv1")
            p.step("sync", lambda e: e.dma_start(out=dbg_kv[:, 2048:4128], in_=vat[:, 0:2080]),
                   deps=[ev_ctx_tile[1]], dma=True, tag="dbgkv2")

        ev_ccs0 = p.step("vector", lambda e: e.memset(ccs[:], 0.0), signal=True, tag="ccs0")
        ev_ctx_evac = []
        for b in range(4):
            p.step("vector", lambda e, b=b: e.tensor_copy(
                out=ccs[:, b * 260:b * 260 + 129], in_=psA[2 + b][:, 0:129]),
                deps=[ev_ctx_tile[NT - 1], ev_ccs0], tag=f"ctxevA{b}")
            ev_ctx_evac.append(p.step("vector", lambda e, b=b: e.tensor_copy(
                out=ccs[:, b * 260 + 130:b * 260 + 259], in_=psA[2 + b][:, 130:259]),
                deps=[ev_ctx_tile[NT - 1]], signal=True, tag=f"ctxevB{b}"))
        ev_cc1_out = p.step("gpsimd", lambda e: e.dma_start(out=cc_ctx_in, in_=ccs[:]),
                            deps=ev_ctx_evac, dma=True, tag="cc1out")
        ev_cc1 = p.step("gpsimd", lambda e: e.collective_compute(
            "AllReduce", ALU.add, replica_groups=[[0, 1], [2, 3], [4, 5], [6, 7]],
            ins=[cc_ctx_in.opt()], outs=[cc_ctx_out.opt()]),
            deps=[ev_cc1_out], cc=True, tag="AR1")
        ev_ctx_rb = p.step("gpsimd", lambda e: e.dma_start(out=ctx_all[:], in_=cc_ctx_out),
                           deps=[ev_cc1], dma=True, tag="cc1rb")
        ev_bd0 = p.step("vector", lambda e: e.memset(ctxbd[:], 0.0), signal=True)
        ev_ks0 = p.step("vector", lambda e: e.memset(ksb[:], 0.0), signal=True)
        ev_ctxbd = []
        ev_ksb = None
        for pr in range(NP):
            bi, off = ctx_bank_off(pr)
            src = (bi - 2) * 260 + off
            p.step("vector", lambda e, pr=pr, src=src: e.tensor_copy(
                out=ctxbd[0:64, pr * 128:pr * 128 + 64], in_=ctx_all[0:64, src:src + 64]),
                deps=[ev_ctx_rb, ev_bd0])
            ev_ctxbd.append(p.step("vector", lambda e, pr=pr, src=src: e.tensor_copy(
                out=ctxbd[64:128, pr * 128 + 64:pr * 128 + 128], in_=ctx_all[64:128, src + 64:src + 128]),
                signal=True, tag=f"ctxbd{pr}"))
            p.step("vector", lambda e, pr=pr, src=src: e.tensor_copy(
                out=ksb[0:64, pr * 16 + 2 * pr:pr * 16 + 2 * pr + 1],
                in_=ctx_all[0:64, src + 128:src + 129]),
                deps=[ev_ks0])
            ev_ksb = p.step("vector", lambda e, pr=pr, src=src: e.tensor_copy(
                out=ksb[64:128, pr * 16 + 2 * pr + 1:pr * 16 + 2 * pr + 2],
                in_=ctx_all[64:128, src + 128:src + 129]),
                signal=True, tag=f"ksb{pr}")

        if dbg:
            p.step("sync", lambda e: e.dma_start(out=dbg_ccs, in_=ccs[:]),
                   deps=[ev_cc1_out], dma=True, tag="dbgccs")
            p.step("sync", lambda e: e.dma_start(out=dbg_ctxall, in_=ctx_all[:]),
                   deps=[ev_ctx_rb], dma=True, tag="dbgctxall")
            p.step("sync", lambda e: e.dma_start(out=dbg_ctxbd[:, 0:1024], in_=ctxbd[:]),
                   deps=[ev_ctxbd[NP - 1]], dma=True, tag="dbgctxbd")

        ev_q_evac = {}
        for nch in range(4):
            for oc in range(KC):
                bank = psA[oc % 2]
                ev = None
                for kc in range(KC):
                    d = [ev_wqkv[kc]]
                    if kc == 0:
                        prev = ev_q_evac.get((nch, oc - 2))
                        if prev is None and nch > 0:
                            prev = ev_q_evac.get((nch - 1, oc + 6))
                        if prev is None:
                            prev = ev_kv_evac[(NT - 1, 2 + (oc % 2))]
                        d.append(prev)
                    ev = p.step("tensor", lambda e, nch=nch, oc=oc, kc=kc, bank=bank: e.matmul(
                        bank[:, 0:512],
                        lhsT=rA[:, kc * 3072 + oc * 128: kc * 3072 + (oc + 1) * 128],
                        rhs=xnT[:, kc * 2048 + nch * 512: kc * 2048 + (nch + 1) * 512],
                        start=(kc == 0), stop=(kc == KC - 1)),
                        deps=d, signal=(kc == KC - 1), tag=f"q{nch}.{oc}")
                ev_q_evac[(nch, oc)] = p.step("scalar", lambda e, nch=nch, oc=oc, bank=bank: e.activation(
                    out=rB[:, oc * 2048 + nch * 512: oc * 2048 + (nch + 1) * 512],
                    in_=bank[:, 0:512], func=AF.Relu),
                    deps=[ev], signal=True, tag=f"qev{nch}.{oc}")
        ev_q_mm_done = ev

        ev_wproj = p.step("gpsimd", lambda e: e.dma_start(
            out=rA[:, 0:8192], in_=wproj_t),
            deps=[ev_q_mm_done], dma=True, tag="wproj")

        ev_zev = [None] * 4
        for ch in range(4):
            ev_dmm2 = None
            for pr in range(NP):
                d = [ev_ksb, ev_q_evac[(ch, pr)]]
                if pr == 0 and ch > 0:
                    d.append(ev_zev[ch - 1])
                ev_dmm2 = p.step("tensor", lambda e, ch=ch, pr=pr: e.matmul(
                    psA[6][0:16, 0:512],
                    lhsT=ksb[:, pr * 16:(pr + 1) * 16],
                    rhs=rB[:, pr * 2048 + ch * 512: pr * 2048 + (ch + 1) * 512],
                    start=(pr == 0), stop=(pr == NP - 1)),
                    deps=d, signal=(pr == NP - 1), tag=f"den{ch}.{pr}")
            ev_zev[ch] = p.step("scalar", lambda e, ch=ch: e.activation(
                out=kat[0:16, ch * 512:(ch + 1) * 512], in_=psA[6][0:16, 0:512],
                func=AF.Copy, bias=Z_EPS),
                deps=[ev_dmm2, ev_ctx_tile[NT - 1] if ch == 0 else None],
                signal=True, tag=f"zev{ch}")
        def _zrec(e):
            with nc.allow_low_precision(reason="z=1/(den+eps): den carries bf16 "
                                        "matmul-input rounding already; bar has 3x margin"):
                return e.reciprocal(out=kat[32:48, 0:2048], in_=kat[0:16, 0:2048])
        ev_zrec = p.step("vector", _zrec, deps=ev_zev, signal=True, tag="zrec")
        ev_ztr = None
        for t in range(NT):
            d = [ev_zrec, ev_idb]
            if t == 0:
                d.append(last_psB[0])
            ev_ztr = p.step("tensor", lambda e, t=t: e.transpose(
                psB16[:, t * 16:(t + 1) * 16],
                in_=kat[32:48, t * 128:(t + 1) * 128],
                identity=ident_b[32:48, 32:48]), deps=d, signal=(t == NT - 1), tag=f"ztr{t}")
        ev_ztok = p.step("vector", lambda e: e.tensor_copy(
            out=ccs[:, 0:256], in_=psB16[:, 0:256]),
            deps=[ev_ztr, ev_cc1_out], signal=True, tag="ztok")
        last_psB[0] = ev_ztok

        ev_att_tt = {}
        ev_att_pair = {}
        ev_a7ev = [None] * NT
        ev_projmm = [None] * NT
        ev_resid = {}

        def emit_att(t):
            b0 = 2 + (t % 2) * 2
            ev_sig = {}
            for pr in range(NP):
                b = b0 + pr // 4
                qq = pr // 4
                d = [ev_ctxbd[pr], ev_q_evac[(t // 4, pr)]]
                if t < 2:
                    d.append(ev_ctx_evac[b - 2])
                else:
                    d.append(ev_att_tt[(t - 2, qq)])
                ev_mm = p.step("tensor", lambda e, t=t, pr=pr, b=b: e.matmul(
                    psA[b][:, (pr % 4) * 128:(pr % 4) * 128 + 128],
                    lhsT=rB[:, pr * 2048 + t * 128: pr * 2048 + (t + 1) * 128],
                    rhs=ctxbd[:, pr * 128:(pr + 1) * 128],
                    start=True, stop=True),
                    deps=d, signal=(pr % 4 == 3), tag=f"att{t}.{pr}")
                if pr % 4 == 3:
                    ev_sig[qq] = ev_mm
            if t == NT - 1:
                for pp in range(4):
                    ev_att_pair[pp] = ev_sig[0]
                    ev_att_pair[4 + pp] = ev_sig[1]
            war_kat = ev_a7ev[t - 2] if t >= 2 else ev_ctx_tile[NT - 1]
            for qq in range(2):
                b = b0 + qq
                ev_att_tt[(t, qq)] = p.step("vector", lambda e, t=t, b=b, qq=qq: e.tensor_tensor(
                    out=kat.ap()[:, (t % 2) * 1024 + qq * 512:(t % 2) * 1024 + (qq + 1) * 512].rearrange(
                        "p (pr h c) -> p pr h c", pr=4, h=2, c=64),
                    in0=psA[b].ap()[:, 0:512].rearrange("p (pr h c) -> p pr h c",
                                                        pr=4, h=2, c=64),
                    in1=ccs.ap()[:, t * 16 + qq * 8: t * 16 + (qq + 1) * 8]
                        .rearrange("p (a c) -> p a c", a=4, c=2)
                        .unsqueeze(3).broadcast_to((128, 4, 2, 64)),
                    op=ALU.mult),
                    deps=[ev_sig[qq], war_kat, ev_ztok], signal=True, tag=f"attTT{t}.{qq}")

        def emit_proj(t):
            ev_tr = None
            for oc in range(KC):
                d = []
                if oc == 0:
                    d = [ev_att_tt[(t, 0)], ev_att_tt[(t, 1)], last_psB[0]]
                ev_tr = p.step("tensor", lambda e, t=t, oc=oc: e.transpose(
                    psB16[:, oc * 128:(oc + 1) * 128],
                    in_=kat[:, (t % 2) * 1024 + oc * 128:(t % 2) * 1024 + (oc + 1) * 128],
                    identity=ident_b[:]), deps=d, signal=(oc == KC - 1), tag=f"a7tr{t}")
            war_vat = ev_projmm[t - 2] if t >= 2 else ev_ctx_tile[NT - 1]
            ev_a7ev[t] = p.step("scalar", lambda e, t=t: e.copy(
                out=vat[:, (t % 2) * 1040:(t % 2) * 1040 + 1024], in_=psB16[:, 0:1024]),
                deps=[ev_tr, war_vat], signal=True, tag=f"a7ev{t}")
            last_psB[0] = ev_a7ev[t]
            ev_pm = {0: None, 1: None}
            for dc in range(KC):
                for hh in range(2):
                    d = [ev_a7ev[t], ev_wproj]
                    if dc == 0:
                        if t == 0:
                            d.append(ev_q_evac[(3, 6 + hh)])
                        else:
                            d.append(ev_resid[(t - 1, hh)])
                    p.step("tensor", lambda e, t=t, dc=dc, hh=hh: e.matmul(
                        psA[hh][:, 0:512],
                        lhsT=vat[:, (t % 2) * 1040 + dc * 128:(t % 2) * 1040 + (dc + 1) * 128],
                        rhs=rA[:, dc * 1024 + hh * 512: dc * 1024 + (hh + 1) * 512],
                        start=(dc == 0), stop=False),
                        deps=d, tag=f"proj{t}.{dc}.{hh}")
            for hh in range(2):
                ev_pm[hh] = p.step("tensor", lambda e, t=t, hh=hh: e.matmul(
                    psA[hh][:, 0:512],
                    lhsT=onesrow_b[:],
                    rhs=biasrow_b[:, hh * 512:(hh + 1) * 512],
                    start=False, stop=True),
                    deps=[ev_orb, ev_bias], signal=True, tag=f"bias{t}.{hh}")
            ev_projmm[t] = ev_pm[1]
            for hh in range(2):
                ev_resid[(t, hh)] = p.step("vector", lambda e, t=t, hh=hh: e.tensor_tensor(
                    out=x_sb[:, t * DM + hh * 512: t * DM + (hh + 1) * 512],
                    in0=psA[hh][:, 0:512],
                    in1=x_sb[:, t * DM + hh * 512: t * DM + (hh + 1) * 512],
                    op=ALU.add),
                    deps=[ev_pm[hh]], signal=True, tag=f"res{t}.{hh}")

        for t in range(NT + 1):
            if t < NT:
                emit_att(t)
            if t >= 1:
                emit_proj(t - 1)

        if dbg:
            p.step("sync", lambda e: e.dma_start(out=dbg_qT, in_=rB[:]),
                   deps=[ev_q_evac[(3, KC - 1)]], dma=True, tag="dbgqT")
            p.step("sync", lambda e: e.dma_start(out=dbg_att, in_=kat[:]),
                   deps=[ev_att_tt[(1, q)] for q in range(2)] + [ev_att_tt[(0, q)] for q in range(2)],
                   dma=True, tag="dbgatt")
            p.step("sync", lambda e: e.dma_start(out=dbg_z, in_=ccs[:, 0:256]),
                   deps=[ev_ztok], dma=True, tag="dbgz")

        if upto == "attn":
            ev_store = []
            for t in range(NT):
                ev_store.append(p.step("sync", lambda e, t=t: e.dma_start(
                    out=y_out[t * TPB:(t + 1) * TPB, :], in_=x_sb[:, t * DM:(t + 1) * DM]),
                    deps=[ev_resid[(t, 0)], ev_resid[(t, 1)]], dma=True, tag=f"st{t}"))
            p.wait_only("sync", ev_store, tag="final")
            p.emit(block, sems)
            return nc

        wd_col = lambda e_: 16384 if e_ % 2 == 0 else 8192
        wd_buf = lambda e_: rA if e_ % 2 == 0 else rB

        def emit_wd_dma(e_, deps):
            return p.step("gpsimd", lambda e, e_=e_: e.dma_start(
                out=wd_buf(e_)[:, wd_col(e_):wd_col(e_) + 8192],
                in_=wd_r[e_]),
                deps=deps, dma=True, tag=f"wd{e_}")

        def emit_wgu_dma(e_, g, deps):
            return p.step("gpsimd", lambda e, e_=e_, g=g: e.dma_start(
                out=rB[:, (g % 2) * 4096:(g % 2) * 4096 + 4096],
                in_=wgu_r[e_, g]),
                deps=deps, dma=True, tag=f"wgu{e_}.{g}")

        ev_wd = [None] * (NE + 1)
        ev_wgu = {}
        ev_wd[0] = emit_wd_dma(0, [ev_q_mm_done, ev_wproj])
        ev_wgu[(0, 0)] = emit_wgu_dma(0, 0, [ev_att_pair[0], ev_att_pair[1]])
        ev_wgu[(0, 1)] = emit_wgu_dma(0, 1, [ev_att_pair[2], ev_att_pair[3]])
        ev_wd[1] = emit_wd_dma(1, [ev_att_pair[4], ev_att_pair[5], ev_att_pair[6],
                                   ev_att_pair[7]])

        ev_xn2T = [None] * NT
        ev_tr2 = [None] * NT
        for t in range(NT):
            war = ev_tr2[t - 2] if t >= 2 else None
            ev_ts = emit_ln(t, [ev_resid[(t, 0)], ev_resid[(t, 1)]], war, f"2.{t}")
            ev_tr2[t], ev_xn2T[t] = emit_transposes(t, ev_ts, f"2.{t}")
        ev_lmm_all = None
        for t in range(NT):
            for kc in range(KC):
                d = [ev_gate, ev_xn2T[t]]
                ev_lmm_all = p.step("tensor", lambda e, t=t, kc=kc: e.matmul(
                    psA[6][:, t * 8:(t + 1) * 8],
                    lhsT=xnT[:, kc * 2048 + t * 128: kc * 2048 + (t + 1) * 128],
                    rhs=gate_sb[:, kc * 8:(kc + 1) * 8],
                    start=(kc == 0), stop=(kc == KC - 1)),
                    deps=d, signal=(t == NT - 1 and kc == KC - 1), tag=f"lg{t}.{kc}")
        lg_all = oh2_scr
        ohbf_all = rA.ap()[:, 8192:8320]
        ev_lgev = p.step("vector", lambda e: e.tensor_copy(out=lg_all[:, 0:128],
                                                           in_=psA[6][:, 0:128]),
                         deps=[ev_lmm_all], signal=True, tag="lgev")
        ev_oh = None
        for t in range(NT):
            p.step("vector", lambda e, t=t: e.max(out=m8[:], in_=lg_all[:, t * 8:(t + 1) * 8]))
            p.step("vector", lambda e, t=t: e.max_index(out=idxu[:], in_max=m8[:],
                                                        in_values=lg_all[:, t * 8:(t + 1) * 8]))
            p.step("vector", lambda e: e.tensor_copy(out=idxf[:], in_=idxu[:]))
            p.step("vector", lambda e, t=t: e.tensor_copy(out=i12a[:, 2 * t:2 * t + 2], in_=idxf[:, 0:2]))
            p.step("vector", lambda e, t=t: e.tensor_tensor(
                out=da[:, t:t + 1], in0=m8[:, 0:1], in1=m8[:, 1:2], op=ALU.subtract))
            ev_oh = p.step("vector", lambda e, t=t: e.tensor_scalar(
                out=ohbf_all[:, t * 8:(t + 1) * 8], in0=cst[:, C_IOTA0:C_IOTA0 + 8],
                scalar1=idxf[:, 0:1], scalar2=None, op0=ALU.is_equal),
                signal=True, tag=f"oh{t}")
        ev_cmm = p.step("tensor", lambda e: e.matmul(
            psA[6][0:1, 128:256], lhsT=onescol_b[:], rhs=ohbf_all,
            start=True, stop=True),
            deps=[ev_oh, ev_ocb], signal=True, tag="cmm")
        ev_ccp = p.step("vector", lambda e: e.tensor_copy(
            out=lg_all[0:1, 0:128], in_=psA[6][0:1, 128:256]),
            deps=[ev_cmm], signal=True, tag="ccp")
        ev_cnt_fin = p.step("vector", lambda e: e.tensor_reduce(
            out=cnt_sb[:],
            in_=lg_all.ap()[0:1, 0:128].rearrange("p (t e) -> p e t", t=NT, e=NE),
            axis=AX.X, op=ALU.add),
            deps=[ev_ccp], signal=True, tag="cntfin")

        ev_cnt_out = p.step("gpsimd", lambda e: e.dma_start(out=cc_cnt_in, in_=cnt_sb[:]),
                            deps=[ev_cnt_fin], dma=True, tag="cc2out")
        ev_cc2 = p.step("gpsimd", lambda e: e.collective_compute(
            "AllReduce", ALU.add, replica_groups=[list(range(N_CORES))],
            ins=[cc_cnt_in.opt()], outs=[cc_cnt_out.opt()]),
            deps=[ev_cnt_out], cc=True, tag="AR2")
        ev_cnt_rb = p.step("gpsimd", lambda e: e.dma_start(out=cnt_row[:], in_=cc_cnt_out),
                           deps=[ev_cc2], dma=True, tag="cc2rb")

        routing_ev = {}

        def emit_routing(war_psA6):
            ev_csum = p.step("tensor", lambda e: e.matmul(
                psA[6][0:NE, 24:25], lhsT=cnt_row[:], rhs=onesrow_f[0:1, 0:1],
                start=True, stop=True),
                deps=[ev_cnt_rb, ev_cnt_fin, ev_orf, war_psA6],
                signal=True, tag="csum")
            ev_c8 = p.step("vector", lambda e: e.tensor_copy(out=cnt8[:, 0:1], in_=psA[6][0:NE, 24:25]),
                           deps=[ev_csum], signal=True)
            p.step("vector", lambda e: e.tensor_scalar(
                out=cnt8[:, 1:2], in0=cnt8[:, 0:1], scalar1=0.0, scalar2=None, op0=ALU.is_gt))
            ev_prim = p.step("vector", lambda e: e.tensor_scalar(
                out=cnt8[:, 4:5], in0=cnt8[:, 1:2], scalar1=1.0, scalar2=-1.0,
                op0=ALU.subtract, op1=ALU.mult), signal=True, tag="nprim")
            ev_prmm = p.step("tensor", lambda e: e.matmul(
                psA[6][0:1, 32:40], lhsT=cnt8[0:NE, 1:2], rhs=ident8[:],
                start=True, stop=True), deps=[ev_prim, ev_i8], signal=True, tag="prmm")
            ev_prow = p.step("vector", lambda e: e.tensor_copy(out=prow[:], in_=psA[6][0:1, 32:40]),
                             deps=[ev_prmm], signal=True)
            ev_pmbm = p.step("tensor", lambda e: e.matmul(
                psA[6][0:NE, 40:48], lhsT=onesrow_f[0:1, 0:NE], rhs=prow[:],
                start=True, stop=True), deps=[ev_prow, ev_orf], signal=True, tag="pmbm")
            p.step("vector", lambda e: e.tensor_copy(out=pmb[:], in_=psA[6][0:NE, 40:48]),
                   deps=[ev_pmbm])
            p.step("vector", lambda e: e.scalar_tensor_tensor(
                out=msim[:], in0=pmb[:], scalar=1e9, in1=sim_sb[:], op0=ALU.mult, op1=ALU.add),
                deps=[ev_simld])
            p.step("vector", lambda e: e.tensor_scalar(
                out=msim[:], in0=msim[:], scalar1=1e9, scalar2=None, op0=ALU.subtract))
            p.step("vector", lambda e: e.max(out=b8[:], in_=msim[:]))
            p.step("vector", lambda e: e.max_index(out=bidxu[:], in_max=b8[:], in_values=msim[:]))
            p.step("vector", lambda e: e.tensor_copy(out=bidxf[:], in_=bidxu[:]))
            p.step("vector", lambda e: e.tensor_scalar(
                out=cnt8[:, 5:6], in0=b8[0:NE, 0:1], scalar1=cst[0:NE, C_THR:C_THR + 1],
                scalar2=None, op0=ALU.is_ge))
            p.step("vector", lambda e: e.tensor_tensor(
                out=cnt8[:, 2:3], in0=cnt8[:, 5:6], in1=cnt8[:, 4:5], op=ALU.mult))
            p.step("vector", lambda e: e.tensor_scalar(
                out=cnt8[:, 6:7], in0=cnt8[:, 2:3], scalar1=1.0, scalar2=-1.0,
                op0=ALU.subtract, op1=ALU.mult))
            p.step("vector", lambda e: e.tensor_tensor(
                out=cnt8[:, 7:8], in0=bidxf[0:NE, 0:1], in1=cnt8[:, 2:3], op=ALU.mult))
            p.step("vector", lambda e: e.tensor_tensor(
                out=cnt8[:, 6:7], in0=cst[0:NE, C_PIDX:C_PIDX + 1], in1=cnt8[:, 6:7], op=ALU.mult))
            ev_em = p.step("vector", lambda e: e.tensor_tensor(
                out=cnt8[:, 3:4], in0=cnt8[:, 7:8], in1=cnt8[:, 6:7], op=ALU.add),
                signal=True, tag="em")
            ev_emm = p.step("tensor", lambda e: e.matmul(
                psA[6][0:1, 48:56], lhsT=cnt8[0:NE, 3:4], rhs=ident8[:],
                start=True, stop=True), deps=[ev_em], signal=True, tag="emm")
            ev_emrow = p.step("vector", lambda e: e.tensor_copy(out=emrow[:], in_=psA[6][0:1, 48:56]),
                              deps=[ev_emm], signal=True)
            ev_embm = p.step("tensor", lambda e: e.matmul(
                psA[6][0:128, 56:64], lhsT=onesrow_f[:], rhs=emrow[:],
                start=True, stop=True), deps=[ev_emrow], signal=True, tag="embm")
            ev_emb = p.step("vector", lambda e: e.tensor_copy(out=emb[:], in_=psA[6][0:128, 56:64]),
                            deps=[ev_embm], signal=True)

            iota_b = cst.ap()[:, C_IOTA0:C_IOTA0 + 8].unsqueeze(1).broadcast_to((128, NT, NE))
            i1_b = i12a.ap().rearrange("p (t k) -> p t k", t=NT, k=2)[:, :, 0:1].broadcast_to((128, NT, NE))
            i2_b = i12a.ap().rearrange("p (t k) -> p t k", t=NT, k=2)[:, :, 1:2].broadcast_to((128, NT, NE))
            oh3 = oh_scr.ap().rearrange("p (t e) -> p t e", t=NT, e=NE)
            oh23 = oh2_scr.ap().rearrange("p (t e) -> p t e", t=NT, e=NE)
            dense3 = dense.ap().rearrange("p (t e) -> p t e", t=NT, e=NE)
            emb_b = emb.ap().unsqueeze(1).broadcast_to((128, NT, NE))
            p.step("vector", lambda e: e.tensor_tensor(out=oh3, in0=iota_b, in1=i1_b, op=ALU.is_equal),
                   deps=[ev_emb])
            p.step("vector", lambda e: e.tensor_tensor(out=oh3, in0=oh3, in1=emb_b, op=ALU.mult))
            p.step("vector", lambda e: e.tensor_reduce(out=r1a[:], in_=oh3, axis=AX.X, op=ALU.add))
            p.step("vector", lambda e: e.tensor_tensor(out=oh23, in0=iota_b, in1=i2_b, op=ALU.is_equal))
            p.step("vector", lambda e: e.tensor_tensor(out=oh23, in0=oh23, in1=emb_b, op=ALU.mult))
            p.step("vector", lambda e: e.tensor_reduce(out=r2a[:], in_=oh23, axis=AX.X, op=ALU.add))
            ev_da_done = p.step("vector", lambda e: e.tensor_copy(out=w2a[:], in_=da[:]), signal=True)
            ev_sig = p.step("scalar", lambda e: e.activation(out=w1a[:], in_=w2a[:], func=AF.Sigmoid),
                            deps=[ev_da_done], signal=True, tag="sigmoid")
            p.step("vector", lambda e: e.tensor_scalar(
                out=w2a[:], in0=w1a[:], scalar1=1.0, scalar2=-1.0, op0=ALU.subtract, op1=ALU.mult),
                deps=[ev_sig])
            r1_b = r1a.ap().unsqueeze(2).broadcast_to((128, NT, NE))
            r2_b = r2a.ap().unsqueeze(2).broadcast_to((128, NT, NE))
            w1_b = w1a.ap().unsqueeze(2).broadcast_to((128, NT, NE))
            w2_b = w2a.ap().unsqueeze(2).broadcast_to((128, NT, NE))
            p.step("vector", lambda e: e.tensor_tensor(out=oh3, in0=iota_b, in1=r1_b, op=ALU.is_equal))
            p.step("vector", lambda e: e.tensor_tensor(out=dense3, in0=oh3, in1=w1_b, op=ALU.mult))
            p.step("vector", lambda e: e.tensor_tensor(out=oh23, in0=iota_b, in1=r2_b, op=ALU.is_equal))
            p.step("vector", lambda e: e.tensor_tensor(out=oh23, in0=oh23, in1=w2_b, op=ALU.mult))
            ev_dense = p.step("vector", lambda e: e.tensor_tensor(
                out=dense3, in0=dense3, in1=oh23, op=ALU.add), signal=True, tag="dense")
            routing_ev["emb"] = ev_emb
            routing_ev["dense"] = ev_dense


        def slot_ap(s, width=512):
            if s < 7:
                return psA[s][:, 0:width]
            return psBf32[:, 0:width]

        ev_gu_fc_last = {}
        ev_sE = {}
        ev_uE = {}
        ev_down_last = [None] * NE
        ev_stt = {}

        last_slot_epilogue = {
            0: ev_resid[(NT - 1, 0)],
            1: ev_resid[(NT - 1, 1)],
            2: ev_att_tt[(NT - 2, 0)],
            3: ev_att_tt[(NT - 2, 1)],
            4: ev_att_tt[(NT - 1, 0)],
            5: ev_att_tt[(NT - 1, 1)],
            6: ev_cnt_fin,
            7: last_psB[0],
        }

        def emit_gu(e_):
            for fc in range(FCG):
                g = fc // 4
                ev_fc = None
                for kc in range(KC):
                    for nch in range(4):
                        s = (fc % 2) * 4 + nch
                        d = [ev_wgu[(e_, g)]]
                        if kc == 0:
                            d.append(last_slot_epilogue.get(s))
                            d.append(ev_xn2T[4 * nch + 3])
                        ev_fc = p.step("tensor", lambda e, fc=fc, kc=kc, nch=nch, s=s: e.matmul(
                            slot_ap(s),
                            lhsT=rB[:, (fc // 4 % 2) * 4096 + (fc % 4) * 1024 + kc * 128:
                                    (fc // 4 % 2) * 4096 + (fc % 4) * 1024 + (kc + 1) * 128],
                            rhs=xnT[:, kc * 2048 + nch * 512: kc * 2048 + (nch + 1) * 512],
                            start=(kc == 0), stop=(kc == KC - 1)),
                            deps=d, signal=(kc == KC - 1), tag=f"gu{e_}.{fc}.{nch}")
                        if kc == KC - 1:
                            if fc < 8:
                                dd = [ev_fc]
                                if e_ > 0:
                                    dd.append(ev_down_last[e_ - 1])
                                ev_sE[(e_, fc, nch)] = p.step("scalar",
                                    lambda e, fc=fc, nch=nch, s=s: e.activation(
                                        out=rA[:, fc * 2048 + nch * 512: fc * 2048 + (nch + 1) * 512],
                                        in_=slot_ap(s), func=AF.Silu),
                                    deps=dd, signal=True, tag=f"silu{e_}.{fc}.{nch}")
                                last_slot_epilogue[s] = ev_sE[(e_, fc, nch)]
                            else:
                                fcc = fc - 8
                                ev_uE[(e_, fcc, nch)] = p.step("vector",
                                    lambda e, fcc=fcc, nch=nch, s=s: e.tensor_tensor(
                                        out=rA[:, fcc * 2048 + nch * 512: fcc * 2048 + (nch + 1) * 512],
                                        in0=rA[:, fcc * 2048 + nch * 512: fcc * 2048 + (nch + 1) * 512],
                                        in1=slot_ap(s), op=ALU.mult),
                                    deps=[ev_fc, ev_sE[(e_, fcc, nch)]], signal=True,
                                    tag=f"umul{e_}.{fcc}.{nch}")
                                last_slot_epilogue[s] = ev_uE[(e_, fcc, nch)]
                ev_gu_fc_last[(e_, fc)] = ev_fc
                if fc == 3:
                    ev_wgu[(e_, 2)] = emit_wgu_dma(e_, 2, [ev_fc])
                elif fc == 7:
                    ev_wgu[(e_, 3)] = emit_wgu_dma(e_, 3, [ev_fc])
                elif fc == 11 and e_ + 1 < NE:
                    ev_wgu[(e_ + 1, 0)] = emit_wgu_dma(e_ + 1, 0, [ev_fc])
                elif fc == 15 and e_ + 1 < NE:
                    ev_wgu[(e_ + 1, 1)] = emit_wgu_dma(e_ + 1, 1, [ev_fc])

        def emit_down(e_):
            ev_dmm = None
            for t in range(NT):
                pA0, pA1 = (0, 1) if t % 2 == 0 else (2, 3)
                for fc in range(KC):
                    for hh in range(2):
                        d = [ev_wd[e_], ev_uE[(e_, fc, t // 4)]]
                        if fc == 0 and hh == 0:
                            if t < 2:
                                d += [last_slot_epilogue[pA0], last_slot_epilogue[pA1]]
                            else:
                                d += [ev_stt[(e_, t - 2, 0)], ev_stt[(e_, t - 2, 1)]]
                        ev_dmm = p.step("tensor",
                            lambda e, e_=e_, t=t, fc=fc, hh=hh, pA=(pA0 if hh == 0 else pA1): e.matmul(
                                psA[pA][:, 0:512],
                                lhsT=rA[:, fc * 2048 + t * 128: fc * 2048 + (t + 1) * 128],
                                rhs=wd_buf(e_)[:, wd_col(e_) + fc * 1024 + hh * 512:
                                               wd_col(e_) + fc * 1024 + (hh + 1) * 512],
                                start=(fc == 0), stop=(fc == KC - 1)),
                            deps=d, signal=(fc == KC - 1), tag=f"dn{e_}.{t}.{hh}")
                for hh in range(2):
                    pA = pA0 if hh == 0 else pA1
                    ev_stt[(e_, t, hh)] = p.step("vector",
                        lambda e, e_=e_, t=t, hh=hh, pA=pA: e.scalar_tensor_tensor(
                            out=x_sb[:, t * DM + hh * 512: t * DM + (hh + 1) * 512],
                            in0=psA[pA][:, 0:512],
                            scalar=dense[:, t * NE + e_: t * NE + e_ + 1],
                            in1=x_sb[:, t * DM + hh * 512: t * DM + (hh + 1) * 512],
                            op0=ALU.mult, op1=ALU.add),
                        deps=[ev_dmm, routing_ev["dense"]], signal=True, tag=f"stt{e_}.{t}.{hh}")
            ev_down_last[e_] = ev_dmm
            last_slot_epilogue[0] = ev_stt[(e_, NT - 2, 0)]
            last_slot_epilogue[1] = ev_stt[(e_, NT - 2, 1)]
            last_slot_epilogue[2] = ev_stt[(e_, NT - 1, 0)]
            last_slot_epilogue[3] = ev_stt[(e_, NT - 1, 1)]
            if e_ + 2 <= NE - 1:
                ev_wd[e_ + 2] = emit_wd_dma(e_ + 2, [ev_down_last[e_]])

        emit_gu(0)
        emit_routing(last_slot_epilogue[6])
        last_slot_epilogue[6] = routing_ev["emb"]
        emit_down(0)
        for e_ in range(1, NE):
            emit_gu(e_)
            emit_down(e_)

        ev_store = []
        for t in range(NT):
            ev_store.append(p.step("sync", lambda e, t=t: e.dma_start(
                out=y_out[t * TPB:(t + 1) * TPB, :], in_=x_sb[:, t * DM:(t + 1) * DM]),
                deps=[ev_stt[(NE - 1, t, 0)], ev_stt[(NE - 1, t, 1)]], dma=True, tag=f"st{t}"))
        p.wait_only("sync", ev_store, tag="final")

        p.emit(block, sems)

    return nc



def _prep_inputs(L0_attn_x, L0_attn_w_qkv, L0_attn_w_proj, L0_attn_b_proj,
                 L0_ffn_gate_weight, L0_ffn_gate_up_proj, L0_ffn_down_proj,
                 L0_ffn_similarity_matrix):
    x = np.ascontiguousarray(
        np.asarray(L0_attn_x, dtype=np.float32).reshape(N_CORES, NT, TPB, DM)
        .transpose(0, 2, 1, 3).reshape(N_CORES, TPB, NT * DM))
    wqkv = np.ascontiguousarray(np.asarray(L0_attn_w_qkv, dtype=np.float32).T).reshape(KC, 128, 3 * DM)
    wproj = np.ascontiguousarray(
        np.asarray(L0_attn_w_proj, dtype=np.float32).T.reshape(KC, 128, DM)
        .transpose(1, 0, 2).reshape(128, KC * DM))
    brow = np.ascontiguousarray(np.asarray(L0_attn_b_proj, dtype=np.float32).reshape(1, DM))
    gw = np.asarray(L0_ffn_gate_weight, dtype=np.float32)
    gate_in = np.ascontiguousarray(gw.T.reshape(KC, 128, NE).transpose(1, 0, 2).reshape(128, KC * NE))
    gu = np.asarray(L0_ffn_gate_up_proj, dtype=np.float32)
    wgu = np.ascontiguousarray(
        gu.reshape(NE, 4, 4, 128, KC, 128).transpose(0, 1, 5, 2, 4, 3)
        .reshape(NE, 4, 128, 4096))
    dn = np.asarray(L0_ffn_down_proj, dtype=np.float32)
    wd = np.ascontiguousarray(
        dn.transpose(0, 2, 1).reshape(NE, KC, 128, DM).transpose(0, 2, 1, 3)
        .reshape(NE, 128, KC * DM))
    sim = np.ascontiguousarray(np.asarray(L0_ffn_similarity_matrix, dtype=np.float32))
    ident = np.eye(128, dtype=np.float32)
    onesrow = np.ones((1, 128), dtype=np.float32)

    in_maps = []
    for c in range(N_CORES):
        cstv = np.zeros((128, C_NCOLS), dtype=np.float32)
        cstv[:, C_LNEPS] = LN_EPS
        cstv[:, C_ZEPS] = Z_EPS
        cstv[:, C_THR] = THR
        cstv[:, C_ONE] = 1.0
        cstv[:, C_IOTA0:C_IOTA0 + 8] = np.arange(8, dtype=np.float32)[None, :]
        cstv[:, C_PIDX] = np.arange(128, dtype=np.float32)
        b = c // 2
        cstv[:, C_CSEL0 + 2 * b] = 1.0
        cstv[:, C_CSEL0 + 2 * b + 1] = 1.0
        in_maps.append({
            "x_in": x[c],
            "wqkv_t": wqkv,
            "wproj_t": wproj,
            "biasrow": brow,
            "gate_in": gate_in,
            "wgu_r": wgu,
            "wd_r": wd,
            "sim_in": sim,
            "cst_in": cstv,
            "ident_in": ident,
            "onesrow_in": onesrow,
        })
    return in_maps


def kernel(L0_attn_x, L0_attn_w_qkv, L0_attn_w_proj, L0_attn_b_proj,
           L0_ffn_gate_weight, L0_ffn_gate_up_proj, L0_ffn_down_proj,
           L0_ffn_similarity_matrix, _upto="full",
           _strict=frozenset(("vector", "scalar"))):
    in_maps = _prep_inputs(L0_attn_x, L0_attn_w_qkv, L0_attn_w_proj, L0_attn_b_proj,
                           L0_ffn_gate_weight, L0_ffn_gate_up_proj, L0_ffn_down_proj,
                           L0_ffn_similarity_matrix)
    nc = build(upto=_upto, strict=_strict)
    res = run_bass_kernel_spmd(nc, in_maps, core_ids=list(range(N_CORES)))
    y = np.concatenate([np.asarray(r["y"], dtype=np.float32) for r in res.results], axis=0)
    return y.reshape(4, 4096, DM)
